# version 39
# baseline (speedup 1.0000x reference)
import math
import contextlib
import numpy as np
import ml_dtypes
import concourse.bass as bass
import concourse.mybir as mybir
from concourse.bass_utils import run_bass_kernel_spmd

F32 = mybir.dt.float32
BF16 = mybir.dt.bfloat16
AF = mybir.ActivationFunctionType
ALU = mybir.AluOpType
AXX = mybir.AxisListType.X

TH_X, SC_X = 0.33, 0.5
TH_W, SC_W = 0.2, 0.6
LN_EPS = 1e-5
C_MARGIN = 10.0

COMPUTE_EKEYS = ("te",)


class Cfg:
    def __init__(self, NH=16, TLOC=2048, RG=4):
        self.NH = NH
        self.DC = NH
        self.D = NH * 128
        self.TLOC = TLOC
        self.RG = RG
        self.TSEQ = RG * TLOC
        self.NTC = TLOC // 128
        self.QHS = TLOC // 2
        self.TTS = min(512, TLOC // 2)
        self.NTT = TLOC // self.TTS
        self.S_SCALE = 1.0 / math.sqrt(128.0)
        self.PBS = min(256, self.DC * self.D)
        self.NPB = (self.DC * self.D) // self.PBS


def _sl(total, step=512):
    o, out = 0, []
    while o < total:
        s = min(step, total - o)
        out.append((o, s))
        o += s
    return out


class Buf:
    __slots__ = ("w", "r")

    def __init__(self):
        self.w = None
        self.r = {}


class Ctx:
    def __init__(self, nc):
        self.nc = nc
        self.sem = {}
        self.cnt = {}
        for name in ("pe", "ac", "dv", "pl"):
            self.sem[name] = nc.alloc_semaphore(name="sem_" + name)
            self.cnt[name] = 0
        self.waited = {}

    def dsem(self, key):
        if key not in self.sem:
            self.sem[key] = self.nc.alloc_semaphore(name="sem_" + key)
            self.cnt[key] = 0
        return key

    def inc(self, instr, s, n=1):
        instr.then_inc(self.sem[s], n)
        self.cnt[s] += n
        return self.cnt[s]

    def wait(self, eng, ekey, s, val):
        if val is None or val <= 0:
            return
        k = (ekey, s)
        if self.waited.get(k, 0) >= val:
            return
        eng.wait_ge(self.sem[s], val)
        self.waited[k] = val

    def _wait_tok(self, eng, qkey, tok):
        tekey, sem, val = tok
        if tekey == qkey and tekey in COMPUTE_EKEYS:
            return
        self.wait(eng, qkey, sem, val)

    def emit(self, fn, eng, ekey, sem, reads=(), writes=(), dn=1, qkey=None):
        qkey = qkey or ekey
        for b in reads:
            if b.w is not None:
                self._wait_tok(eng, qkey, b.w)
        for b in writes:
            if b.w is not None:
                self._wait_tok(eng, qkey, b.w)
            for t in list(b.r.values()):
                self._wait_tok(eng, qkey, t)
        instr = fn()
        tok = self.inc(instr, sem, dn)
        t = (ekey, sem, tok)
        for b in reads:
            b.r[ekey] = t
        for b in writes:
            b.w = t
            b.r = {}
        return tok

    def emit_dma(self, fn, eng, qkey, key, reads=(), writes=()):
        self.dsem(key)
        return self.emit(fn, eng, "dq_" + key, key, reads=reads,
                         writes=writes, dn=16, qkey=qkey)

    def barrier_all(self, engs):
        snap = dict(self.cnt)
        for eng, ekey in engs:
            for s in snap:
                self.wait(eng, ekey, s, snap[s])

    def snap_guard(self):
        g = Buf()
        for s in ("pe", "ac", "dv", "pl"):
            g.r["snap_" + s] = ("snap_" + s, s, self.cnt[s])
        return g

    @staticmethod
    def guard(*bufs):
        g = Buf()
        i = 0
        for b in bufs:
            if b.w is not None:
                g.r[f"g{i}"] = b.w
                i += 1
            for t in b.r.values():
                g.r[f"g{i}"] = t
                i += 1
        return g


def build(cfg: Cfg):
    nc = bass.Bass("TRN2", target_bir_lowering=False)
    NH, DC, D, TLOC = cfg.NH, cfg.DC, cfg.D, cfg.TLOC
    RG, NTC, QHS, TTS, NTT = cfg.RG, cfg.NTC, cfg.QHS, cfg.TTS, cfg.NTT
    PBS, NPB, TSEQ = cfg.PBS, cfg.NPB, cfg.TSEQ
    EVW = max(D, TLOC)
    PJW = min(TLOC, 1024)

    xT = nc.dram_tensor("xT", [D, TLOC], F32, kind="ExternalInput")
    xkT = nc.dram_tensor("xkT", [D, TSEQ], F32, kind="ExternalInput")
    wq = nc.dram_tensor("wq", [128, DC * D], F32, kind="ExternalInput")
    wk = nc.dram_tensor("wk", [128, DC * D], F32, kind="ExternalInput")
    wv = nc.dram_tensor("wv", [128, DC * D], F32, kind="ExternalInput")
    wo = nc.dram_tensor("wo", [128, DC * D], F32, kind="ExternalInput")
    wf = nc.dram_tensor("wf", [128, DC * D], F32, kind="ExternalInput")
    wr = nc.dram_tensor("wr", [128, DC * D], F32, kind="ExternalInput")
    outT = nc.dram_tensor("outT", [D, TLOC], F32, kind="ExternalOutput")

    K_loc = nc.dram_tensor("K_loc", [D, TSEQ], BF16)
    q_loc = nc.dram_tensor("q_loc", [D, TLOC], BF16)
    V_loc = nc.dram_tensor("V_loc", [TSEQ, D], BF16)
    wr_loc = nc.dram_tensor("wr_loc", [128, DC * D], BF16)

    TE, SC, VE, GP, SY = nc.tensor, nc.scalar, nc.vector, nc.gpsimd, nc.sync
    c = Ctx(nc)
    c.psg = Buf()
    ENGS = [(TE, "te"), (SC, "sc"), (VE, "ve"), (GP, "pl"), (SY, "sy")]
    es = contextlib.ExitStack()

    def SB(name, shape, dtype, side, stack=None):
        st = stack if stack is not None else es
        return st.enter_context(nc.sbuf_tensor(name, shape, dtype, side=side))

    wstage = [SB(f"wstage{i}", [128, D], F32, "left") for i in range(3)]
    wbf = [SB(f"wbf{i}", [128, D], BF16, "left") for i in range(2)]
    ones_col_f = SB("ones_col_f", [128, 1], F32, "left")
    ones_col_b = SB("ones_col_b", [128, 1], BF16, "left")
    ones_row_f = SB("ones_row_f", [1, 128], F32, "left")
    negc_sb = SB("negc_sb", [128, 1], F32, "left")
    sc1_sb = SB("sc1_sb", [1, 1], F32, "left")
    sc2_sb = SB("sc2_sb", [1, 1], F32, "left")
    eps_sb = SB("eps_sb", [1, 1], F32, "left")
    thn_sb = SB("thn_sb", [128, 1], F32, "left")
    thp_sb = SB("thp_sb", [128, 1], F32, "left")
    xthn_sb = SB("xthn_sb", [128, 1], F32, "left")
    xthp_sb = SB("xthp_sb", [128, 1], F32, "left")
    zer_sb = SB("zer_sb", [1, 1], F32, "left")

    b_wstage = [Buf(), Buf(), Buf()]
    b_wbf = [Buf(), Buf()]
    b_negc = Buf()
    b_sc1, b_sc2 = Buf(), Buf()
    b_ones = Buf()

    c.emit(lambda: VE.memset(ones_col_f.ap(), 1.0), VE, "ve", "dv",
           writes=(b_ones,))
    c.emit(lambda: VE.memset(ones_col_b.ap(), 1.0), VE, "ve", "dv",
           writes=(b_ones,))
    c.emit(lambda: VE.memset(ones_row_f.ap(), 1.0), VE, "ve", "dv",
           writes=(b_ones,))
    c.emit(lambda: VE.memset(eps_sb.ap(), LN_EPS), VE, "ve", "dv",
           writes=(b_ones,))
    c.emit(lambda: VE.memset(thn_sb.ap(), -TH_W), VE, "ve", "dv",
           writes=(b_ones,))
    c.emit(lambda: VE.memset(thp_sb.ap(), TH_W), VE, "ve", "dv",
           writes=(b_ones,))
    c.emit(lambda: VE.memset(xthn_sb.ap(), -TH_X), VE, "ve", "dv",
           writes=(b_ones,))
    c.emit(lambda: VE.memset(xthp_sb.ap(), TH_X), VE, "ve", "dv",
           writes=(b_ones,))
    c.emit(lambda: VE.memset(zer_sb.ap(), 0.0), VE, "ve", "dv",
           writes=(b_ones,))

    es_rmain = contextlib.ExitStack()
    xbf = SB("xbf", [128, DC * TLOC], BF16, "right", es_rmain)
    evac = [SB(f"evac{i}", [128, EVW], BF16, "right", es_rmain)
            for i in range(2)]
    xstage = [SB(f"xstage{i}", [128, TLOC], F32, "right", es_rmain)
              for i in range(3)]
    b_xbf = Buf()
    b_evac = [Buf(), Buf()]
    b_xstage = [Buf() for _ in range(3)]

    def load_cast_x(src_dram, col0):
        for dc in range(DC):
            pr = dc % 3
            c.emit_dma(lambda dc=dc, pr=pr: SY.dma_start(
                out=xstage[pr].ap(),
                in_=src_dram[dc * 128:(dc + 1) * 128, col0:col0 + TLOC]),
                SY, "sy", f"xst{pr}", writes=(b_xstage[pr],))
            if dc % 2 == 0:
                c.emit(lambda dc=dc, pr=pr: VE.tensor_copy(
                    xbf[:, dc * TLOC:(dc + 1) * TLOC], xstage[pr].ap()),
                    VE, "ve", "dv", reads=(b_xstage[pr],), writes=(b_xbf,))
            else:
                c.emit(lambda dc=dc, pr=pr: SC.activation(
                    xbf[:, dc * TLOC:(dc + 1) * TLOC], xstage[pr].ap(),
                    AF.Copy),
                    SC, "sc", "ac", reads=(b_xstage[pr],), writes=(b_xbf,))

    es_p = contextlib.ExitStack()
    fstage = [SB(f"fstage{i}", [128, PBS], F32, "right", es_p)
              for i in range(4)]
    rstage = [SB(f"rstage{i}", [128, PBS], F32, "right", es_p)
              for i in range(4)]
    ft1 = SB("ft1", [128, PBS], F32, "right", es_p)
    ft2 = SB("ft2", [128, PBS], F32, "right", es_p)
    fout = [SB(f"fout{i}", [128, PBS], BF16, "right", es_p) for i in range(2)]
    b_fstage = [Buf() for _ in range(4)]
    b_rstage = [Buf() for _ in range(4)]
    b_ft1, b_ft2 = Buf(), Buf()
    b_fout = [Buf(), Buf()]
    b_wrloc = Buf()

    def prep_load(b):
        pq = b % 4
        c.emit_dma(lambda b=b, pq=pq: GP.dma_start(
            out=fstage[pq].ap(), in_=wf[:, b * PBS:(b + 1) * PBS]),
            GP, "gq", f"fst{pq}", writes=(b_fstage[pq],))
        c.emit_dma(lambda b=b, pq=pq: GP.dma_start(
            out=rstage[pq].ap(), in_=wr[:, b * PBS:(b + 1) * PBS]),
            GP, "gq", f"rst{pq}", writes=(b_rstage[pq],))

    def prep_compute(b):
        pq = b % 4
        pr = b % 2
        c.emit(lambda pq=pq: SC.activation(ft1.ap(), fstage[pq].ap(),
                                           AF.Sign, bias=thn_sb.ap()),
               SC, "sc", "ac", reads=(b_fstage[pq], b_ones), writes=(b_ft1,))
        c.emit(lambda pq=pq: SC.activation(ft2.ap(), fstage[pq].ap(),
                                           AF.Sign, bias=thp_sb.ap()),
               SC, "sc", "ac", reads=(b_fstage[pq], b_ones), writes=(b_ft2,))
        c.emit(lambda: VE.tensor_add(ft1.ap(), ft1.ap(), ft2.ap()),
               VE, "ve", "dv", reads=(b_ft2,), writes=(b_ft1,))
        c.emit(lambda pq=pq, pr=pr: VE.scalar_tensor_tensor(
            fout[pr].ap(), ft1.ap(), SC_W / 2.0, rstage[pq].ap(),
            ALU.mult, ALU.add),
            VE, "ve", "dv", reads=(b_ft1, b_rstage[pq]),
            writes=(b_fout[pr],))
        c.emit_dma(lambda b=b, pr=pr: GP.dma_start(
            out=wr_loc[:, b * PBS:(b + 1) * PBS], in_=fout[pr].ap()),
            GP, "gq", f"fwr{pr}", reads=(b_fout[pr],), writes=(b_wrloc,))

    prep_state = [0, 0]

    def drain_prep(n):
        for _ in range(n):
            while prep_state[0] < min(NPB, prep_state[1] + 4):
                prep_load(prep_state[0])
                prep_state[0] += 1
            if prep_state[1] < NPB:
                prep_compute(prep_state[1])
                prep_state[1] += 1

    es_v = contextlib.ExitStack()
    wvbf = SB("wvbf", [128, DC * D], BF16, "right", es_v)
    b_wvbf = Buf()
    for b in range(DC):
        pr = b % 3
        c.emit_dma(lambda b=b, pr=pr: SY.dma_start(
            out=wstage[pr].ap(), in_=wv[:, b * D:(b + 1) * D]),
            SY, "sy", f"wst{pr}", writes=(b_wstage[pr],))
        c.emit(lambda b=b, pr=pr: VE.tensor_copy(
            wvbf[:, b * D:(b + 1) * D], wstage[pr].ap()),
            VE, "ve", "dv", reads=(b_wstage[pr],), writes=(b_wvbf,))

    b_vloc = [Buf(), Buf()]
    b_kloc = [Buf(), Buf()]
    b_qloc = [Buf(), Buf()]

    def v_subphase(kc):
        with nc.psum_tensor(f"V{kc}_p0", [128, max(512, D)], F32) as Vp0, \
             nc.psum_tensor(f"V{kc}_p1", [128, max(512, D)], F32) as Vp1:
            V_ps = [Vp0, Vp1]
            b_vps = [Buf(), Buf()]
            for tr in range(NTC):
                pr = tr % 2

                def mm(tr=tr, pr=pr):
                    last = None
                    for dc in range(DC):
                        lhsT = xbf[:, dc * TLOC + tr * 128:
                                   dc * TLOC + (tr + 1) * 128]
                        for (e0, el) in _sl(D):
                            last = TE.matmul(V_ps[pr][:, e0:e0 + el], lhsT,
                                             wvbf[:, dc * D + e0:
                                                  dc * D + e0 + el],
                                             start=(dc == 0),
                                             stop=(dc == DC - 1))
                    return last
                c.emit(mm, TE, "te", "pe", reads=(b_xbf, b_wvbf),
                       writes=((b_vps[pr], c.psg) if tr == 0
                               else (b_vps[pr],)))
                c.emit(lambda pr=pr: SC.activation(evac[pr][:, 0:D],
                                                   V_ps[pr][:, 0:D],
                                                   AF.Copy),
                       SC, "sc", "ac", reads=(b_vps[pr],),
                       writes=(b_evac[pr],))
                row = kc * TLOC + tr * 128
                c.emit_dma(lambda row=row, pr=pr: GP.dma_start(
                    out=V_loc[row:row + 128, :], in_=evac[pr][:, 0:D]),
                    GP, "gq", f"vst{pr}", reads=(b_evac[pr],),
                    writes=(b_vloc[pr],))
            c.psg = c.snap_guard()

    def proj_sub(kc, wdram, is_q, dest_q=None, b_destq=None):
        tag = ("qp" if is_q else f"kp{kc}")
        with nc.psum_tensor(f"{tag}_p0", [128, max(512, PJW)], F32) as Pj0, \
             nc.psum_tensor(f"{tag}_p1", [128, max(512, PJW)], F32) as Pj1:
            P_ps = [Pj0, Pj1]
            b_pps = [Buf(), Buf()]

            def wload(h):
                ps3 = h % 3
                c.emit_dma(lambda h=h, ps3=ps3: SC.dma_start(
                    out=wstage[ps3].ap(), in_=wdram[:, h * D:(h + 1) * D]),
                    SC, "sc", f"wst{ps3}", writes=(b_wstage[ps3],))

            for h in range(min(3, NH)):
                wload(h)
            for h in range(NH):
                pr = h % 2
                ps3 = h % 3
                c.emit(lambda pr=pr, ps3=ps3: VE.tensor_copy(
                    wbf[pr].ap(), wstage[ps3].ap()),
                    VE, "ve", "dv", reads=(b_wstage[ps3],),
                    writes=(b_wbf[pr],))
                if h + 3 < NH:
                    wload(h + 3)
                ev_ap = evac[pr][:, 0:TLOC]
                b_ev = b_evac[pr]
                for thi, (th0, thl) in enumerate(_sl(TLOC, PJW)):
                    pj = (h * 2 + thi) % 2
                    def mm(h=h, pr=pr, pj=pj, th0=th0, thl=thl):
                        last = None
                        for dc in range(DC):
                            lhsT = wbf[pr][:, dc * 128:(dc + 1) * 128]
                            for (t0, tl) in _sl(thl):
                                last = TE.matmul(
                                    P_ps[pj][:, t0:t0 + tl], lhsT,
                                    xbf[:, dc * TLOC + th0 + t0:
                                        dc * TLOC + th0 + t0 + tl],
                                    start=(dc == 0), stop=(dc == DC - 1))
                        return last
                    c.emit(mm, TE, "te", "pe", reads=(b_wbf[pr], b_xbf),
                           writes=((b_pps[pj], c.psg) if h == 0 and th0 == 0
                                   else (b_pps[pj],)))
                    c.emit(lambda pj=pj, ev_ap=ev_ap, th0=th0, thl=thl:
                           SC.activation(ev_ap[:, th0:th0 + thl],
                                         P_ps[pj][:, 0:thl], AF.Copy),
                           SC, "sc", "ac", reads=(b_pps[pj],), writes=(b_ev,))
                if is_q:
                    c.emit_dma(lambda h=h, pr=pr, ev_ap=ev_ap:
                               GP.dma_start(
                        out=q_loc[h * 128:(h + 1) * 128, :], in_=ev_ap),
                        GP, "gq", f"qst{pr}", reads=(b_ev,),
                        writes=(b_qloc[pr],))
                else:
                    c.emit_dma(lambda h=h, kc=kc, pr=pr, ev_ap=ev_ap:
                               GP.dma_start(
                        out=K_loc[h * 128:(h + 1) * 128,
                                  kc * TLOC:(kc + 1) * TLOC], in_=ev_ap),
                        GP, "gq", f"kst{pr}", reads=(b_ev,),
                        writes=(b_kloc[pr],))
                drain_prep(2)
            c.psg = c.snap_guard()

    for kc in range(RG):
        load_cast_x(xkT, kc * TLOC)
        proj_sub(kc, wk, False)
        v_subphase(kc)
    es_v.close()

    es_qa = contextlib.ExitStack()
    load_cast_x(xkT, 0)
    proj_sub(0, wq, True)
    drain_prep(NPB)
    assert prep_state[1] == NPB
    es_p.close()

    QSAMP = min(512, TLOC)
    es_ks = contextlib.ExitStack()
    ksamp_sb = SB("ksamp_sb", [128, NH * 128], BF16, "left", es_ks)
    qsmp_sb = [SB(f"qsmp{i}", [128, QSAMP], BF16, "left", es_ks)
               for i in range(2)]
    b_ksamp = Buf()
    b_qsmp = [Buf(), Buf()]
    c.emit_dma(lambda: SY.dma_start(
        out=ksamp_sb.ap().rearrange("p (h k) -> p h k", k=128),
        in_=K_loc[:, 0:128].rearrange("(h p) k -> p h k", p=128)),
        SY, "sy", "ksmp", reads=(b_kloc[0], b_kloc[1]), writes=(b_ksamp,))
    with nc.psum_tensor("smp0", [128, 512], F32) as Smp0, \
         nc.psum_tensor("smp1", [128, 512], F32) as Smp1:
        SMP = [Smp0, Smp1]
        b_smp = [Buf(), Buf()]
        for h in range(NH):
            sp = h % 2
            c.emit_dma(lambda h=h, sp=sp: SY.dma_start(
                out=qsmp_sb[sp].ap(),
                in_=q_loc[h * 128:(h + 1) * 128, 0:QSAMP]),
                SY, "sy", f"qsm{sp}",
                reads=(b_qloc[0], b_qloc[1]), writes=(b_qsmp[sp],))
            c.emit(lambda h=h, sp=sp: TE.matmul(
                SMP[sp][0:1, 0:QSAMP], ksamp_sb[:, h * 128:h * 128 + 1],
                qsmp_sb[sp].ap(), start=True, stop=True),
                TE, "te", "pe",
                reads=(b_ksamp, b_qsmp[sp]),
                writes=((b_smp[sp], c.psg) if h == 0 else (b_smp[sp],)))
            c.emit(lambda sp=sp: VE.reduce_max(sc1_sb.ap(),
                                               SMP[sp][0:1, 0:QSAMP],
                                               axis=AXX),
                   VE, "ve", "dv", reads=(b_smp[sp],), writes=(b_sc1,))
            if h == 0:
                c.emit(lambda: VE.tensor_copy(sc2_sb.ap(), sc1_sb.ap()),
                       VE, "ve", "dv", reads=(b_sc1,), writes=(b_sc2,))
            else:
                c.emit(lambda: VE.tensor_max(sc2_sb.ap(), sc2_sb.ap(),
                                             sc1_sb.ap()),
                       VE, "ve", "dv", reads=(b_sc1,), writes=(b_sc2,))
        c.psg = c.snap_guard()
    c.emit(lambda: VE.tensor_scalar(sc1_sb.ap(), sc2_sb.ap(), cfg.S_SCALE,
                                    C_MARGIN, ALU.mult, ALU.add),
           VE, "ve", "dv", reads=(b_sc2,), writes=(b_sc1,))
    c.emit(lambda: VE.tensor_scalar(sc1_sb.ap(), sc1_sb.ap(), -1.0, None,
                                    ALU.mult),
           VE, "ve", "dv", writes=(b_sc1,))
    with nc.psum_tensor("ncp", [128, 512], F32) as NCP:
        b_ncp = Buf()
        c.emit(lambda: TE.matmul(NCP[:, 0:1], ones_row_f.ap(), sc1_sb.ap(),
                                 start=True, stop=True),
               TE, "te", "pe", reads=(b_sc1, b_ones), writes=(b_ncp, c.psg))
        c.emit(lambda: VE.tensor_copy(negc_sb.ap(), NCP[:, 0:1]),
               VE, "ve", "dv", reads=(b_ncp,), writes=(b_negc,))
        c.psg = c.snap_guard()

    c.barrier_all(ENGS)
    es_ks.close()

    es_rmain.close()
    attn_sb = SB("attn_sb", [128, NH * TLOC], BF16, "right")
    Q_buf = [SB(f"Q_buf{i}", [128, TLOC], BF16, "right", es_qa)
             for i in range(2)]
    K_buf = [SB(f"K_buf{i}", [128, TLOC], BF16, "right", es_qa)
             for i in range(3)]
    V_buf = [SB(f"V_buf{i}", [128, NTC * 128], BF16, "right", es_qa)
             for i in range(3)]
    P_sb = [SB(f"P_sb{i}", [128, QHS], BF16, "right", es_qa)
            for i in range(3)]
    racc = SB("racc", [128, 2 * TLOC], F32, "right", es_qa)
    rb_sb = [SB(f"rb_sb{i}", [128, QHS], F32, "right", es_qa)
             for i in range(2)]
    recip_sb = [SB(f"recip_sb{i}", [1, QHS], F32, "right", es_qa)
                for i in range(2)]

    b_qbuf = [Buf(), Buf()]
    b_kbuf = [Buf(), Buf(), Buf()]
    b_vbuf = [Buf(), Buf(), Buf()]
    b_psb = [Buf(), Buf(), Buf()]
    b_racc = [Buf(), Buf(), Buf(), Buf()]
    b_rb = [Buf(), Buf()]
    b_recip = [Buf(), Buf()]
    b_attn = Buf()

    v_view = V_loc.rearrange("(r tc p) e -> r p tc e", tc=NTC, p=128)

    with nc.psum_tensor("S_ps0", [128, max(512, QHS)], F32) as Sp0, \
         nc.psum_tensor("S_ps1", [128, max(512, QHS)], F32) as Sp1, \
         nc.psum_tensor("O_ps0", [128, max(512, QHS)], F32) as Op0, \
         nc.psum_tensor("O_ps1", [128, max(512, QHS)], F32) as Op1:
        S_ps = [Sp0, Sp1]
        O_ps = [Op0, Op1]
        b_sps = [Buf(), Buf()]
        b_ops = [Buf(), Buf()]
        NBLK = NH * RG

        def emit_kv_load(blk):
            h, r = blk // RG, blk % RG
            pr = blk % 3
            c.emit_dma(lambda h=h, r=r, pr=pr: SY.dma_start(
                out=K_buf[pr].ap(),
                in_=K_loc[h * 128:(h + 1) * 128, r * TLOC:(r + 1) * TLOC]),
                SY, "sy", f"kb{pr}", reads=(b_kloc[0], b_kloc[1]),
                writes=(b_kbuf[pr],))
            c.emit_dma(lambda h=h, r=r, pr=pr: SY.dma_start(
                out=V_buf[pr].ap().rearrange("p (tc e) -> p tc e", e=128),
                in_=v_view[r, :, :, h * 128:(h + 1) * 128]),
                SY, "sy", f"vb{pr}", reads=(b_vloc[0], b_vloc[1]),
                writes=(b_vbuf[pr],))

        def emit_q_load(h):
            qp = h % 2
            c.emit_dma(lambda h=h, qp=qp: SY.dma_start(
                out=Q_buf[qp].ap(), in_=q_loc[h * 128:(h + 1) * 128, :]),
                SY, "sy", f"qb{qp}",
                reads=(b_qloc[0], b_qloc[1]), writes=(b_qbuf[qp],))

        emit_kv_load(0)
        emit_kv_load(1)
        emit_kv_load(2)
        emit_q_load(0)
        emit_q_load(1)

        pend = []

        def flush_one():
            if pend:
                pv_fn, racc_fn = pend.pop(0)
                pv_fn()
                racc_fn()

        def flush_pend():
            while pend:
                flush_one()

        u = 0
        for h in range(NH):
            for r in range(RG):
                blk = h * RG + r
                first_of_blk = True
                kb = blk % 3
                for qh in range(2):
                    for tc in range(NTC):
                        j = r * NTC + tc
                        up = u % 3
                        sp2 = u % 2

                        qp = h % 2

                        def mm_s(h=h, tc=tc, qh=qh, sp2=sp2, kb=kb, qp=qp):
                            last = None
                            for (q0, ql) in _sl(QHS):
                                last = TE.matmul(
                                    S_ps[sp2][:, q0:q0 + ql],
                                    K_buf[kb][:, tc * 128:(tc + 1) * 128],
                                    Q_buf[qp][:, qh * QHS + q0:
                                              qh * QHS + q0 + ql],
                                    start=True, stop=True)
                            return last
                        c.emit(mm_s, TE, "te", "pe",
                               reads=(b_kbuf[kb], b_qbuf[qp]),
                               writes=((b_sps[sp2], c.psg) if u == 0
                                       else (b_sps[sp2],)))
                        c.emit(lambda up=up, sp2=sp2: SC.activation(
                            P_sb[up].ap(), S_ps[sp2][:, 0:QHS], AF.Exp,
                            bias=negc_sb.ap(), scale=cfg.S_SCALE),
                            SC, "sc", "ac", reads=(b_sps[sp2], b_negc),
                            writes=(b_psb[up],))
                        if len(pend) >= 2:
                            flush_one()
                        if first_of_blk:
                            first_of_blk = False
                            if 3 <= blk + 2 < NBLK + 1 and blk + 2 < NBLK:
                                flush_pend()
                                emit_kv_load(blk + 2)

                        def pv_fn(j=j, tc=tc, qh=qh, up=up, kb=kb):
                            def mm_pv():
                                last = None
                                for (q0, ql) in _sl(QHS):
                                    last = TE.matmul(
                                        O_ps[qh][:, q0:q0 + ql],
                                        V_buf[kb][:, tc * 128:(tc + 1) * 128],
                                        P_sb[up][:, q0:q0 + ql],
                                        start=(j == 0),
                                        stop=(j == RG * NTC - 1))
                                return last
                            c.emit(mm_pv, TE, "te", "pe",
                                   reads=(b_vbuf[kb], b_psb[up]),
                                   writes=(b_ops[qh],))

                        if (u % 4) != 3:
                            eng, ekey, sem = VE, "ve", "dv"
                        else:
                            eng, ekey, sem = GP, "pl", "pl"

                        ai = qh * 2 + (j % 2)

                        def racc_fn(eng=eng, ekey=ekey, sem=sem, j=j, ai=ai,
                                    up=up):
                            sl_ = racc[:, ai * QHS:(ai + 1) * QHS]
                            if j < 2:
                                c.emit(lambda: eng.tensor_copy(
                                    sl_, P_sb[up].ap()),
                                    eng, ekey, sem, reads=(b_psb[up],),
                                    writes=(b_racc[ai],))
                            else:
                                c.emit(lambda: eng.tensor_add(
                                    sl_, sl_, P_sb[up].ap()),
                                    eng, ekey, sem, reads=(b_psb[up],),
                                    writes=(b_racc[ai],))

                        pend.append((pv_fn, racc_fn))
                        u += 1
            flush_pend()
            if h + 2 < NH:
                emit_q_load(h + 2)

            def st_rs(qh, h=h):
                sp = qh

                def mm_rs():
                    last = None
                    for (q0, ql) in _sl(QHS):
                        for pi in range(2):
                            ai = qh * 2 + pi
                            last = TE.matmul(S_ps[sp][0:1, q0:q0 + ql],
                                             ones_col_f.ap(),
                                             racc[:, ai * QHS + q0:
                                                  ai * QHS + q0 + ql],
                                             start=(pi == 0), stop=(pi == 1))
                    return last
                c.emit(mm_rs, TE, "te", "pe",
                       reads=(b_racc[qh * 2], b_racc[qh * 2 + 1], b_ones),
                       writes=(b_sps[sp],))

            def st_cb(qh, h=h):
                sp = qh
                c.emit(lambda: SC.activation(
                    recip_sb[qh].ap(), S_ps[sp][0:1, 0:QHS], AF.Ln,
                    bias=zer_sb.ap()),
                    SC, "sc", "ac", reads=(b_sps[sp], b_ones),
                    writes=(b_recip[qh],))
                c.emit(lambda: SC.activation(
                    recip_sb[qh].ap(), recip_sb[qh].ap(), AF.Exp,
                    scale=-1.0),
                    SC, "sc", "ac", reads=(b_recip[qh],),
                    writes=(b_recip[qh],))

                def mm_bc():
                    last = None
                    for (q0, ql) in _sl(QHS):
                        last = TE.matmul(S_ps[sp][:, q0:q0 + ql],
                                         ones_row_f.ap(),
                                         recip_sb[qh][0:1, q0:q0 + ql],
                                         start=True, stop=True)
                    return last
                c.emit(mm_bc, TE, "te", "pe", reads=(b_recip[qh], b_ones),
                       writes=(b_sps[sp],))

            def st_rm(qh, h=h):
                sp = qh
                c.emit(lambda: VE.tensor_copy(
                    rb_sb[qh].ap(), S_ps[sp][:, 0:QHS]),
                    VE, "ve", "dv", reads=(b_sps[sp],), writes=(b_rb[qh],))
                dst = attn_sb[:, h * TLOC + qh * QHS:
                              h * TLOC + (qh + 1) * QHS]
                c.emit(lambda dst=dst: VE.tensor_mul(
                    dst, O_ps[qh][:, 0:QHS], rb_sb[qh].ap()),
                    VE, "ve", "dv", reads=(b_ops[qh], b_rb[qh]),
                    writes=(b_attn,))

            st_rs(0)
            st_rs(1)
            st_cb(0)
            st_cb(1)
            st_rm(0)
            st_rm(1)
        c.psg = c.snap_guard()

    es_qa.close()
    c.barrier_all(ENGS)

    x2_sb2 = [SB(f"x2_sb{i}", [128, DC * TTS], F32, "left")
              for i in range(2)]
    x2q_sb = SB("x2q_sb", [128, DC * TTS], BF16, "left")
    acc_sb = SB("acc_sb", [128, TTS], F32, "left")
    acc2_sb = SB("acc2_sb", [128, TTS], F32, "left")
    sqt_sb = SB("sqt_sb", [128, TTS], F32, "left")
    qt1_sb = SB("qt1_sb", [128, TTS], BF16, "left")
    qt2_sb = SB("qt2_sb", [128, TTS], BF16, "left")
    xres = [SB(f"xres{i}", [128, TTS], F32, "left") for i in range(2)]
    ab_sb = SB("ab_sb", [128, TTS], F32, "left")
    bb_sb = SB("bb_sb", [128, TTS], F32, "left")
    m_sb = SB("m_sb", [1, TTS], F32, "left")
    v_sb = SB("v_sb", [1, TTS], F32, "left")
    r_sb = SB("r_sb", [1, TTS], F32, "left")
    bsm_sb = SB("bsm_sb", [1, TTS], F32, "left")

    b_x22 = [[Buf() for _ in range(DC)] for _ in range(2)]
    b_x2q = Buf()
    b_acc, b_acc2, b_sqt = Buf(), Buf(), Buf()
    b_qt1, b_qt2 = Buf(), Buf()
    b_xres = [Buf(), Buf()]
    b_ab, b_bb = Buf(), Buf()
    b_m, b_v, b_r, b_bsm = Buf(), Buf(), Buf(), Buf()
    b_outb = [Buf(), Buf()]

    with nc.psum_tensor("M_ps0", [128, max(512, TTS)], F32) as Mp0, \
         nc.psum_tensor("M_ps1", [128, max(512, TTS)], F32) as Mp1, \
         nc.psum_tensor("L_ps0", [128, max(512, TTS)], F32) as Lp0, \
         nc.psum_tensor("L_ps1", [128, max(512, TTS)], F32) as Lp1:
        M_ps = [Mp0, Mp1]
        L_ps = [Lp0, Lp1]
        b_mps = [Buf(), Buf()]
        b_lps = [Buf(), Buf()]

        def ln_finalize():
            def mm_sum():
                last = None
                for (t0, tl) in _sl(TTS):
                    last = TE.matmul(L_ps[0][0:1, t0:t0 + tl],
                                     ones_col_f.ap(), acc_sb[:, t0:t0 + tl],
                                     start=True, stop=True)
                return last

            def mm_sum2():
                last = None
                for (t0, tl) in _sl(TTS):
                    last = TE.matmul(L_ps[1][0:1, t0:t0 + tl],
                                     ones_col_f.ap(), acc2_sb[:, t0:t0 + tl],
                                     start=True, stop=True)
                return last
            c.emit(mm_sum, TE, "te", "pe", reads=(b_acc, b_ones),
                   writes=(b_lps[0],))
            c.emit(mm_sum2, TE, "te", "pe", reads=(b_acc2, b_ones),
                   writes=(b_lps[1],))
            c.emit(lambda: VE.tensor_scalar(m_sb.ap(), L_ps[0][0:1, 0:TTS],
                                            1.0 / D, None, ALU.mult),
                   VE, "ve", "dv", reads=(b_lps[0],), writes=(b_m,))
            c.emit(lambda: VE.tensor_scalar(v_sb.ap(), L_ps[1][0:1, 0:TTS],
                                            1.0 / D, None, ALU.mult),
                   VE, "ve", "dv", reads=(b_lps[1],), writes=(b_v,))
            c.emit(lambda: VE.tensor_mul(bsm_sb.ap(), m_sb.ap(), m_sb.ap()),
                   VE, "ve", "dv", reads=(b_m,), writes=(b_bsm,))
            c.emit(lambda: VE.tensor_sub(v_sb.ap(), v_sb.ap(), bsm_sb.ap()),
                   VE, "ve", "dv", reads=(b_bsm,), writes=(b_v,))
            c.emit(lambda: SC.activation(r_sb.ap(), v_sb.ap(), AF.Ln,
                                         bias=eps_sb.ap()),
                   SC, "sc", "ac", reads=(b_v, b_ones), writes=(b_r,))
            c.emit(lambda: SC.activation(r_sb.ap(), r_sb.ap(), AF.Exp,
                                         scale=-0.5),
                   SC, "sc", "ac", reads=(b_r,), writes=(b_r,))
            c.emit(lambda: VE.scalar_tensor_tensor(
                bsm_sb.ap(), m_sb.ap(), -1.0, r_sb.ap(), ALU.mult, ALU.mult),
                VE, "ve", "dv", reads=(b_m, b_r), writes=(b_bsm,))

            def mm_ba():
                last = None
                for (t0, tl) in _sl(TTS):
                    last = TE.matmul(L_ps[0][:, t0:t0 + tl], ones_row_f.ap(),
                                     r_sb[0:1, t0:t0 + tl],
                                     start=True, stop=True)
                return last

            def mm_bb():
                last = None
                for (t0, tl) in _sl(TTS):
                    last = TE.matmul(L_ps[1][:, t0:t0 + tl], ones_row_f.ap(),
                                     bsm_sb[0:1, t0:t0 + tl],
                                     start=True, stop=True)
                return last
            c.emit(mm_ba, TE, "te", "pe", reads=(b_r, b_ones),
                   writes=(b_lps[0],))
            c.emit(mm_bb, TE, "te", "pe", reads=(b_bsm, b_ones),
                   writes=(b_lps[1],))
            c.emit(lambda: SC.activation(ab_sb.ap(), L_ps[0][:, 0:TTS],
                                         AF.Copy),
                   SC, "sc", "ac", reads=(b_lps[0],), writes=(b_ab,))
            c.emit(lambda: SC.activation(bb_sb.ap(), L_ps[1][:, 0:TTS],
                                         AF.Copy),
                   SC, "sc", "ac", reads=(b_lps[1],), writes=(b_bb,))

        for tt in range(NTT):
            x2_sb = x2_sb2[tt % 2]
            b_x2 = b_x22[tt % 2]
            for ec in range(DC):
                pr = ec % 2
                ps3 = ec % 3
                c.emit_dma(lambda ec=ec, ps3=ps3: SY.dma_start(
                    out=wstage[ps3].ap(), in_=wo[:, ec * D:(ec + 1) * D]),
                    SY, "sy", f"wst{ps3}", writes=(b_wstage[ps3],))
                c.emit(lambda pr=pr, ps3=ps3: SC.activation(
                    wbf[pr].ap(), wstage[ps3].ap(), AF.Copy),
                    SC, "sc", "ac", reads=(b_wstage[ps3],),
                    writes=(b_wbf[pr],))
                c.emit_dma(lambda ec=ec, tt=tt, pr=pr: SY.dma_start(
                    out=xres[pr].ap(),
                    in_=xT[ec * 128:(ec + 1) * 128,
                           tt * TTS:(tt + 1) * TTS]),
                    SY, "sy", f"xrs{pr}", writes=(b_xres[pr],))

                def mm(ec=ec, tt=tt, pr=pr):
                    last = None
                    for dc in range(DC):
                        lhsT = wbf[pr][:, dc * 128:(dc + 1) * 128]
                        for (t0, tl) in _sl(TTS):
                            last = TE.matmul(
                                M_ps[pr][:, t0:t0 + tl], lhsT,
                                attn_sb[:, dc * TLOC + tt * TTS + t0:
                                        dc * TLOC + tt * TTS + t0 + tl],
                                start=(dc == 0), stop=(dc == DC - 1))
                    return last
                c.emit(mm, TE, "te", "pe", reads=(b_wbf[pr], b_attn),
                       writes=((b_mps[pr], c.psg) if tt == 0 and ec == 0
                               else (b_mps[pr],)))
                dst = x2_sb[:, ec * TTS:(ec + 1) * TTS]
                c.emit(lambda pr=pr, dst=dst: VE.tensor_add(
                    dst, M_ps[pr][:, 0:TTS], xres[pr].ap()),
                    VE, "ve", "dv", reads=(b_mps[pr], b_xres[pr]),
                    writes=(b_x2[ec],))
                if ec == 0:
                    c.emit(lambda dst=dst: VE.tensor_copy(acc_sb.ap(), dst),
                           VE, "ve", "dv", reads=(b_x2[ec],), writes=(b_acc,))
                else:
                    c.emit(lambda dst=dst: VE.tensor_add(acc_sb.ap(),
                                                         acc_sb.ap(), dst),
                           VE, "ve", "dv", reads=(b_x2[ec],), writes=(b_acc,))
                c.emit(lambda dst=dst: VE.tensor_mul(sqt_sb.ap(), dst, dst),
                       VE, "ve", "dv", reads=(b_x2[ec],), writes=(b_sqt,))
                if ec == 0:
                    c.emit(lambda: VE.tensor_copy(acc2_sb.ap(), sqt_sb.ap()),
                           VE, "ve", "dv", reads=(b_sqt,), writes=(b_acc2,))
                else:
                    c.emit(lambda: VE.tensor_add(acc2_sb.ap(), acc2_sb.ap(),
                                                 sqt_sb.ap()),
                           VE, "ve", "dv", reads=(b_sqt,), writes=(b_acc2,))
            ln_finalize()
            for ec in range(DC):
                dst = x2_sb[:, ec * TTS:(ec + 1) * TTS]
                c.emit(lambda dst=dst: VE.tensor_mul(dst, dst, ab_sb.ap()),
                       VE, "ve", "dv", reads=(b_ab,), writes=(b_x2[ec],))
                c.emit(lambda dst=dst: VE.tensor_add(dst, dst, bb_sb.ap()),
                       VE, "ve", "dv", reads=(b_bb,), writes=(b_x2[ec],))
                c.emit(lambda dst=dst: SC.activation(
                    qt1_sb.ap(), dst, AF.Sign, bias=xthn_sb.ap()),
                    SC, "sc", "ac", reads=(b_x2[ec], b_ones), writes=(b_qt1,))
                c.emit(lambda dst=dst: SC.activation(
                    qt2_sb.ap(), dst, AF.Sign, bias=xthp_sb.ap()),
                    SC, "sc", "ac", reads=(b_x2[ec], b_ones), writes=(b_qt2,))
                c.emit(lambda: VE.tensor_add(qt1_sb.ap(), qt1_sb.ap(),
                                             qt2_sb.ap()),
                       VE, "ve", "dv", reads=(b_qt2,), writes=(b_qt1,))
                qdst = x2q_sb[:, ec * TTS:(ec + 1) * TTS]
                c.emit(lambda qdst=qdst: SC.activation(
                    qdst, qt1_sb.ap(), AF.Copy, scale=SC_X / 2.0),
                    SC, "sc", "ac", reads=(b_qt1,), writes=(b_x2q,))

            for oc in range(DC):
                pr = oc % 2
                c.emit_dma(lambda oc=oc, pr=pr: SY.dma_start(
                    out=wbf[pr].ap(), in_=wr_loc[:, oc * D:(oc + 1) * D]),
                    SY, "sy", f"wrb{pr}", reads=(b_wrloc,),
                    writes=(b_wbf[pr],))

                def mm(oc=oc, pr=pr):
                    last = None
                    for dc in range(DC):
                        lhsT = wbf[pr][:, dc * 128:(dc + 1) * 128]
                        for (t0, tl) in _sl(TTS):
                            last = TE.matmul(
                                M_ps[pr][:, t0:t0 + tl], lhsT,
                                x2q_sb[:, dc * TTS + t0: dc * TTS + t0 + tl],
                                start=(dc == 0), stop=(dc == DC - 1))
                    return last
                c.emit(mm, TE, "te", "pe", reads=(b_wbf[pr], b_x2q),
                       writes=(b_mps[pr],))
                dst = x2_sb[:, oc * TTS:(oc + 1) * TTS]
                c.emit(lambda pr=pr, dst=dst: VE.tensor_add(
                    dst, M_ps[pr][:, 0:TTS], dst),
                    VE, "ve", "dv", reads=(b_mps[pr],), writes=(b_x2[oc],))
                if oc == 0:
                    c.emit(lambda dst=dst: VE.tensor_copy(acc_sb.ap(), dst),
                           VE, "ve", "dv", reads=(b_x2[oc],), writes=(b_acc,))
                else:
                    c.emit(lambda dst=dst: VE.tensor_add(acc_sb.ap(),
                                                         acc_sb.ap(), dst),
                           VE, "ve", "dv", reads=(b_x2[oc],), writes=(b_acc,))
                c.emit(lambda dst=dst: VE.tensor_mul(sqt_sb.ap(), dst, dst),
                       VE, "ve", "dv", reads=(b_x2[oc],), writes=(b_sqt,))
                if oc == 0:
                    c.emit(lambda: VE.tensor_copy(acc2_sb.ap(), sqt_sb.ap()),
                           VE, "ve", "dv", reads=(b_sqt,), writes=(b_acc2,))
                else:
                    c.emit(lambda: VE.tensor_add(acc2_sb.ap(), acc2_sb.ap(),
                                                 sqt_sb.ap()),
                           VE, "ve", "dv", reads=(b_sqt,), writes=(b_acc2,))
            ln_finalize()
            for oc in range(DC):
                dst = x2_sb[:, oc * TTS:(oc + 1) * TTS]
                c.emit(lambda dst=dst: VE.tensor_mul(dst, dst, ab_sb.ap()),
                       VE, "ve", "dv", reads=(b_ab,), writes=(b_x2[oc],))
                c.emit(lambda dst=dst: VE.tensor_add(dst, dst, bb_sb.ap()),
                       VE, "ve", "dv", reads=(b_bb,), writes=(b_x2[oc],))
                c.emit_dma(lambda oc=oc, tt=tt, dst=dst: GP.dma_start(
                    out=outT[oc * 128:(oc + 1) * 128,
                             tt * TTS:(tt + 1) * TTS],
                    in_=dst),
                    GP, "gq", f"out{oc % 2}", reads=(b_x2[oc],),
                    writes=(b_outb[oc % 2],))

    c.barrier_all(ENGS)
    es.close()
    return nc


def _reorder_hblock(WT, DC):
    d, e = WT.shape
    nh = e // 128
    a = WT.reshape(DC, 128, nh, 128).transpose(1, 2, 0, 3)
    return np.ascontiguousarray(a.reshape(128, nh * DC * 128))


def _reorder_plain(WT, DC):
    d, e = WT.shape
    a = WT.reshape(DC, 128, e).transpose(1, 0, 2)
    return np.ascontiguousarray(a.reshape(128, DC * e))


def kernel(L0_attn_x, L0_attn_Wq, L0_attn_Wk, L0_attn_Wv, L0_attn_Wo,
           L0_ffn_weight, L0_ffn_rram_variation):
    cfg = Cfg(NH=16, TLOC=2048, RG=4)
    x = np.asarray(L0_attn_x, dtype=np.float32)
    B, T, Dm = x.shape
    f32 = lambda a: np.asarray(a, dtype=np.float32)
    ins = {
        "wq": _reorder_hblock(f32(L0_attn_Wq).T, cfg.DC),
        "wk": _reorder_hblock(f32(L0_attn_Wk).T, cfg.DC),
        "wv": _reorder_plain(f32(L0_attn_Wv).T, cfg.DC),
        "wo": _reorder_hblock(f32(L0_attn_Wo).T, cfg.DC),
        "wf": _reorder_hblock(f32(L0_ffn_weight).T, cfg.DC),
        "wr": _reorder_hblock(f32(L0_ffn_rram_variation).T, cfg.DC),
    }
    nc = build(cfg)
    xkTs = [np.ascontiguousarray(x[b].T) for b in range(B)]
    in_maps = []
    ncores = 2 * cfg.RG
    for core in range(ncores):
        b, sl = core // cfg.RG, core % cfg.RG
        m = dict(ins)
        m["xT"] = np.ascontiguousarray(
            x[b, sl * cfg.TLOC:(sl + 1) * cfg.TLOC, :].T)
        xk = xkTs[b]
        s = sl * cfg.TLOC
        m["xkT"] = (xk if s == 0 else np.ascontiguousarray(
            np.concatenate([xk[:, s:], xk[:, :s]], axis=1)))
        in_maps.append(m)
    res = run_bass_kernel_spmd(nc, in_maps, core_ids=list(range(ncores)))
    out = np.empty((B, T, Dm), dtype=np.float32)
    for core in range(ncores):
        b, sl = core // cfg.RG, core % cfg.RG
        out[b, sl * cfg.TLOC:(sl + 1) * cfg.TLOC, :] = \
            res.results[core]["outT"].T
    return out


# version 41
# speedup vs baseline: 1.0088x; 1.0088x over previous
import math
import contextlib
import numpy as np
import ml_dtypes
import concourse.bass as bass
import concourse.mybir as mybir
from concourse.bass_utils import run_bass_kernel_spmd

F32 = mybir.dt.float32
BF16 = mybir.dt.bfloat16
AF = mybir.ActivationFunctionType
ALU = mybir.AluOpType
AXX = mybir.AxisListType.X

TH_X, SC_X = 0.33, 0.5
TH_W, SC_W = 0.2, 0.6
LN_EPS = 1e-5
C_MARGIN = 10.0

COMPUTE_EKEYS = ("te",)


class Cfg:
    def __init__(self, NH=16, TLOC=2048, RG=4):
        self.NH = NH
        self.DC = NH
        self.D = NH * 128
        self.TLOC = TLOC
        self.RG = RG
        self.TSEQ = RG * TLOC
        self.NTC = TLOC // 128
        self.QHS = TLOC // 2
        self.TTS = min(512, TLOC // 2)
        self.NTT = TLOC // self.TTS
        self.S_SCALE = 1.0 / math.sqrt(128.0)
        self.PBS = min(256, self.DC * self.D)
        self.NPB = (self.DC * self.D) // self.PBS


def _sl(total, step=512):
    o, out = 0, []
    while o < total:
        s = min(step, total - o)
        out.append((o, s))
        o += s
    return out


class Buf:
    __slots__ = ("w", "r")

    def __init__(self):
        self.w = None
        self.r = {}


class Ctx:
    def __init__(self, nc):
        self.nc = nc
        self.sem = {}
        self.cnt = {}
        for name in ("pe", "ac", "dv", "pl"):
            self.sem[name] = nc.alloc_semaphore(name="sem_" + name)
            self.cnt[name] = 0
        self.waited = {}

    def dsem(self, key):
        if key not in self.sem:
            self.sem[key] = self.nc.alloc_semaphore(name="sem_" + key)
            self.cnt[key] = 0
        return key

    def inc(self, instr, s, n=1):
        instr.then_inc(self.sem[s], n)
        self.cnt[s] += n
        return self.cnt[s]

    def wait(self, eng, ekey, s, val):
        if val is None or val <= 0:
            return
        k = (ekey, s)
        if self.waited.get(k, 0) >= val:
            return
        eng.wait_ge(self.sem[s], val)
        self.waited[k] = val

    def _wait_tok(self, eng, qkey, tok):
        tekey, sem, val = tok
        if tekey == qkey and tekey in COMPUTE_EKEYS:
            return
        self.wait(eng, qkey, sem, val)

    def emit(self, fn, eng, ekey, sem, reads=(), writes=(), dn=1, qkey=None):
        qkey = qkey or ekey
        for b in reads:
            if b.w is not None:
                self._wait_tok(eng, qkey, b.w)
        for b in writes:
            if b.w is not None:
                self._wait_tok(eng, qkey, b.w)
            for t in list(b.r.values()):
                self._wait_tok(eng, qkey, t)
        instr = fn()
        tok = self.inc(instr, sem, dn)
        t = (ekey, sem, tok)
        for b in reads:
            b.r[ekey] = t
        for b in writes:
            b.w = t
            b.r = {}
        return tok

    def emit_dma(self, fn, eng, qkey, key, reads=(), writes=()):
        self.dsem(key)
        return self.emit(fn, eng, "dq_" + key, key, reads=reads,
                         writes=writes, dn=16, qkey=qkey)

    def barrier_all(self, engs):
        snap = dict(self.cnt)
        for eng, ekey in engs:
            for s in snap:
                self.wait(eng, ekey, s, snap[s])

    def snap_guard(self):
        g = Buf()
        for s in ("pe", "ac", "dv", "pl"):
            g.r["snap_" + s] = ("snap_" + s, s, self.cnt[s])
        return g

    @staticmethod
    def guard(*bufs):
        g = Buf()
        i = 0
        for b in bufs:
            if b.w is not None:
                g.r[f"g{i}"] = b.w
                i += 1
            for t in b.r.values():
                g.r[f"g{i}"] = t
                i += 1
        return g


def build(cfg: Cfg):
    nc = bass.Bass("TRN2", target_bir_lowering=False)
    NH, DC, D, TLOC = cfg.NH, cfg.DC, cfg.D, cfg.TLOC
    RG, NTC, QHS, TTS, NTT = cfg.RG, cfg.NTC, cfg.QHS, cfg.TTS, cfg.NTT
    PBS, NPB, TSEQ = cfg.PBS, cfg.NPB, cfg.TSEQ
    EVW = max(D, TLOC)
    PJW = min(TLOC, 1024)

    xT = nc.dram_tensor("xT", [D, TLOC], F32, kind="ExternalInput")
    xkT = nc.dram_tensor("xkT", [D, TSEQ], F32, kind="ExternalInput")
    wq = nc.dram_tensor("wq", [128, DC * D], F32, kind="ExternalInput")
    wk = nc.dram_tensor("wk", [128, DC * D], F32, kind="ExternalInput")
    wv = nc.dram_tensor("wv", [128, DC * D], F32, kind="ExternalInput")
    wo = nc.dram_tensor("wo", [128, DC * D], F32, kind="ExternalInput")
    wf = nc.dram_tensor("wf", [128, DC * D], F32, kind="ExternalInput")
    wr = nc.dram_tensor("wr", [128, DC * D], F32, kind="ExternalInput")
    outT = nc.dram_tensor("outT", [D, TLOC], F32, kind="ExternalOutput")

    K_loc = nc.dram_tensor("K_loc", [D, TSEQ], BF16)
    q_loc = nc.dram_tensor("q_loc", [D, TLOC], BF16)
    V_loc = nc.dram_tensor("V_loc", [TSEQ, D], BF16)
    wr_loc = nc.dram_tensor("wr_loc", [128, DC * D], BF16)

    TE, SC, VE, GP, SY = nc.tensor, nc.scalar, nc.vector, nc.gpsimd, nc.sync
    c = Ctx(nc)
    c.psg = Buf()
    ENGS = [(TE, "te"), (SC, "sc"), (VE, "ve"), (GP, "pl"), (SY, "sy")]
    es = contextlib.ExitStack()

    def SB(name, shape, dtype, side, stack=None):
        st = stack if stack is not None else es
        return st.enter_context(nc.sbuf_tensor(name, shape, dtype, side=side))

    wstage = [SB(f"wstage{i}", [128, D], F32, "left") for i in range(3)]
    wbf = [SB(f"wbf{i}", [128, D], BF16, "left") for i in range(2)]
    ones_col_f = SB("ones_col_f", [128, 1], F32, "left")
    ones_col_b = SB("ones_col_b", [128, 1], BF16, "left")
    ones_row_f = SB("ones_row_f", [1, 128], F32, "left")
    negc_sb = SB("negc_sb", [128, 1], F32, "left")
    sc1_sb = SB("sc1_sb", [1, 1], F32, "left")
    sc2_sb = SB("sc2_sb", [1, 1], F32, "left")
    eps_sb = SB("eps_sb", [1, 1], F32, "left")
    thn_sb = SB("thn_sb", [128, 1], F32, "left")
    thp_sb = SB("thp_sb", [128, 1], F32, "left")
    xthn_sb = SB("xthn_sb", [128, 1], F32, "left")
    xthp_sb = SB("xthp_sb", [128, 1], F32, "left")
    zer_sb = SB("zer_sb", [1, 1], F32, "left")

    b_wstage = [Buf(), Buf(), Buf()]
    b_wbf = [Buf(), Buf()]
    b_negc = Buf()
    b_sc1, b_sc2 = Buf(), Buf()
    b_ones = Buf()

    c.emit(lambda: VE.memset(ones_col_f.ap(), 1.0), VE, "ve", "dv",
           writes=(b_ones,))
    c.emit(lambda: VE.memset(ones_col_b.ap(), 1.0), VE, "ve", "dv",
           writes=(b_ones,))
    c.emit(lambda: VE.memset(ones_row_f.ap(), 1.0), VE, "ve", "dv",
           writes=(b_ones,))
    c.emit(lambda: VE.memset(eps_sb.ap(), LN_EPS), VE, "ve", "dv",
           writes=(b_ones,))
    c.emit(lambda: VE.memset(thn_sb.ap(), -TH_W), VE, "ve", "dv",
           writes=(b_ones,))
    c.emit(lambda: VE.memset(thp_sb.ap(), TH_W), VE, "ve", "dv",
           writes=(b_ones,))
    c.emit(lambda: VE.memset(xthn_sb.ap(), -TH_X), VE, "ve", "dv",
           writes=(b_ones,))
    c.emit(lambda: VE.memset(xthp_sb.ap(), TH_X), VE, "ve", "dv",
           writes=(b_ones,))
    c.emit(lambda: VE.memset(zer_sb.ap(), 0.0), VE, "ve", "dv",
           writes=(b_ones,))

    es_rmain = contextlib.ExitStack()
    xbf = SB("xbf", [128, DC * TLOC], BF16, "right", es_rmain)
    evac = [SB(f"evac{i}", [128, EVW], BF16, "right", es_rmain)
            for i in range(2)]
    xstage = [SB(f"xstage{i}", [128, TLOC], F32, "right", es_rmain)
              for i in range(3)]
    b_xbf = Buf()
    b_evac = [Buf(), Buf()]
    b_xstage = [Buf() for _ in range(3)]

    def load_cast_x(src_dram, col0):
        for dc in range(DC):
            pr = dc % 3
            c.emit_dma(lambda dc=dc, pr=pr: SY.dma_start(
                out=xstage[pr].ap(),
                in_=src_dram[dc * 128:(dc + 1) * 128, col0:col0 + TLOC]),
                SY, "sy", f"xst{pr}", writes=(b_xstage[pr],))
            if dc % 2 == 0:
                c.emit(lambda dc=dc, pr=pr: VE.tensor_copy(
                    xbf[:, dc * TLOC:(dc + 1) * TLOC], xstage[pr].ap()),
                    VE, "ve", "dv", reads=(b_xstage[pr],), writes=(b_xbf,))
            else:
                c.emit(lambda dc=dc, pr=pr: SC.activation(
                    xbf[:, dc * TLOC:(dc + 1) * TLOC], xstage[pr].ap(),
                    AF.Copy),
                    SC, "sc", "ac", reads=(b_xstage[pr],), writes=(b_xbf,))

    es_p = contextlib.ExitStack()
    fstage = [SB(f"fstage{i}", [128, PBS], F32, "right", es_p)
              for i in range(4)]
    rstage = [SB(f"rstage{i}", [128, PBS], F32, "right", es_p)
              for i in range(4)]
    ft1 = SB("ft1", [128, PBS], F32, "right", es_p)
    ft2 = SB("ft2", [128, PBS], F32, "right", es_p)
    fout = [SB(f"fout{i}", [128, PBS], BF16, "right", es_p) for i in range(2)]
    b_fstage = [Buf() for _ in range(4)]
    b_rstage = [Buf() for _ in range(4)]
    b_ft1, b_ft2 = Buf(), Buf()
    b_fout = [Buf(), Buf()]
    b_wrloc = Buf()

    def prep_load(b):
        pq = b % 4
        c.emit_dma(lambda b=b, pq=pq: GP.dma_start(
            out=fstage[pq].ap(), in_=wf[:, b * PBS:(b + 1) * PBS]),
            GP, "gq", f"fst{pq}", writes=(b_fstage[pq],))
        c.emit_dma(lambda b=b, pq=pq: GP.dma_start(
            out=rstage[pq].ap(), in_=wr[:, b * PBS:(b + 1) * PBS]),
            GP, "gq", f"rst{pq}", writes=(b_rstage[pq],))

    def prep_compute(b):
        pq = b % 4
        pr = b % 2
        c.emit(lambda pq=pq: SC.activation(ft1.ap(), fstage[pq].ap(),
                                           AF.Sign, bias=thn_sb.ap()),
               SC, "sc", "ac", reads=(b_fstage[pq], b_ones), writes=(b_ft1,))
        c.emit(lambda pq=pq: SC.activation(ft2.ap(), fstage[pq].ap(),
                                           AF.Sign, bias=thp_sb.ap()),
               SC, "sc", "ac", reads=(b_fstage[pq], b_ones), writes=(b_ft2,))
        c.emit(lambda: VE.tensor_add(ft1.ap(), ft1.ap(), ft2.ap()),
               VE, "ve", "dv", reads=(b_ft2,), writes=(b_ft1,))
        c.emit(lambda pq=pq, pr=pr: VE.scalar_tensor_tensor(
            fout[pr].ap(), ft1.ap(), SC_W / 2.0, rstage[pq].ap(),
            ALU.mult, ALU.add),
            VE, "ve", "dv", reads=(b_ft1, b_rstage[pq]),
            writes=(b_fout[pr],))
        c.emit_dma(lambda b=b, pr=pr: GP.dma_start(
            out=wr_loc[:, b * PBS:(b + 1) * PBS], in_=fout[pr].ap()),
            GP, "gq", f"fwr{pr}", reads=(b_fout[pr],), writes=(b_wrloc,))

    prep_state = [0, 0]

    def drain_prep(n):
        for _ in range(n):
            while prep_state[0] < min(NPB, prep_state[1] + 4):
                prep_load(prep_state[0])
                prep_state[0] += 1
            if prep_state[1] < NPB:
                prep_compute(prep_state[1])
                prep_state[1] += 1

    es_v = contextlib.ExitStack()
    wvbf = SB("wvbf", [128, DC * D], BF16, "right", es_v)
    b_wvbf = Buf()
    for b in range(DC):
        pr = b % 3
        c.emit_dma(lambda b=b, pr=pr: SY.dma_start(
            out=wstage[pr].ap(), in_=wv[:, b * D:(b + 1) * D]),
            SY, "sy", f"wst{pr}", writes=(b_wstage[pr],))
        c.emit(lambda b=b, pr=pr: VE.tensor_copy(
            wvbf[:, b * D:(b + 1) * D], wstage[pr].ap()),
            VE, "ve", "dv", reads=(b_wstage[pr],), writes=(b_wvbf,))

    b_vloc = [Buf(), Buf()]
    b_kloc = [Buf(), Buf()]
    b_qloc = [Buf(), Buf()]

    def v_subphase(kc):
        with nc.psum_tensor(f"V{kc}_p0", [128, max(512, D)], F32) as Vp0, \
             nc.psum_tensor(f"V{kc}_p1", [128, max(512, D)], F32) as Vp1:
            V_ps = [Vp0, Vp1]
            b_vps = [Buf(), Buf()]
            for tr in range(NTC):
                pr = tr % 2

                def mm(tr=tr, pr=pr):
                    last = None
                    for dc in range(DC):
                        lhsT = xbf[:, dc * TLOC + tr * 128:
                                   dc * TLOC + (tr + 1) * 128]
                        for (e0, el) in _sl(D):
                            last = TE.matmul(V_ps[pr][:, e0:e0 + el], lhsT,
                                             wvbf[:, dc * D + e0:
                                                  dc * D + e0 + el],
                                             start=(dc == 0),
                                             stop=(dc == DC - 1))
                    return last
                c.emit(mm, TE, "te", "pe", reads=(b_xbf, b_wvbf),
                       writes=((b_vps[pr], c.psg) if tr == 0
                               else (b_vps[pr],)))
                c.emit(lambda pr=pr: SC.activation(evac[pr][:, 0:D],
                                                   V_ps[pr][:, 0:D],
                                                   AF.Copy),
                       SC, "sc", "ac", reads=(b_vps[pr],),
                       writes=(b_evac[pr],))
                row = kc * TLOC + tr * 128
                c.emit_dma(lambda row=row, pr=pr: GP.dma_start(
                    out=V_loc[row:row + 128, :], in_=evac[pr][:, 0:D]),
                    GP, "gq", f"vst{pr}", reads=(b_evac[pr],),
                    writes=(b_vloc[pr],))
            c.psg = c.snap_guard()

    def proj_sub(kc, wdram, is_q, dest_q=None, b_destq=None):
        tag = ("qp" if is_q else f"kp{kc}")
        with nc.psum_tensor(f"{tag}_p0", [128, max(512, PJW)], F32) as Pj0, \
             nc.psum_tensor(f"{tag}_p1", [128, max(512, PJW)], F32) as Pj1:
            P_ps = [Pj0, Pj1]
            b_pps = [Buf(), Buf()]

            def wload(h):
                ps3 = h % 3
                c.emit_dma(lambda h=h, ps3=ps3: SC.dma_start(
                    out=wstage[ps3].ap(), in_=wdram[:, h * D:(h + 1) * D]),
                    SC, "sc", f"wst{ps3}", writes=(b_wstage[ps3],))

            for h in range(min(3, NH)):
                wload(h)
            for h in range(NH):
                pr = h % 2
                ps3 = h % 3
                c.emit(lambda pr=pr, ps3=ps3: VE.tensor_copy(
                    wbf[pr].ap(), wstage[ps3].ap()),
                    VE, "ve", "dv", reads=(b_wstage[ps3],),
                    writes=(b_wbf[pr],))
                if h + 3 < NH:
                    wload(h + 3)
                ev_ap = evac[pr][:, 0:TLOC]
                b_ev = b_evac[pr]
                for thi, (th0, thl) in enumerate(_sl(TLOC, PJW)):
                    pj = (h * 2 + thi) % 2
                    def mm(h=h, pr=pr, pj=pj, th0=th0, thl=thl):
                        last = None
                        for dc in range(DC):
                            lhsT = wbf[pr][:, dc * 128:(dc + 1) * 128]
                            for (t0, tl) in _sl(thl):
                                last = TE.matmul(
                                    P_ps[pj][:, t0:t0 + tl], lhsT,
                                    xbf[:, dc * TLOC + th0 + t0:
                                        dc * TLOC + th0 + t0 + tl],
                                    start=(dc == 0), stop=(dc == DC - 1))
                        return last
                    c.emit(mm, TE, "te", "pe", reads=(b_wbf[pr], b_xbf),
                           writes=((b_pps[pj], c.psg) if h == 0 and th0 == 0
                                   else (b_pps[pj],)))
                    c.emit(lambda pj=pj, ev_ap=ev_ap, th0=th0, thl=thl:
                           SC.activation(ev_ap[:, th0:th0 + thl],
                                         P_ps[pj][:, 0:thl], AF.Copy),
                           SC, "sc", "ac", reads=(b_pps[pj],), writes=(b_ev,))
                if is_q:
                    c.emit_dma(lambda h=h, pr=pr, ev_ap=ev_ap:
                               GP.dma_start(
                        out=q_loc[h * 128:(h + 1) * 128, :], in_=ev_ap),
                        GP, "gq", f"qst{pr}", reads=(b_ev,),
                        writes=(b_qloc[pr],))
                else:
                    c.emit_dma(lambda h=h, kc=kc, pr=pr, ev_ap=ev_ap:
                               GP.dma_start(
                        out=K_loc[h * 128:(h + 1) * 128,
                                  kc * TLOC:(kc + 1) * TLOC], in_=ev_ap),
                        GP, "gq", f"kst{pr}", reads=(b_ev,),
                        writes=(b_kloc[pr],))
                drain_prep(2)
            c.psg = c.snap_guard()

    for kc in range(RG):
        load_cast_x(xkT, kc * TLOC)
        proj_sub(kc, wk, False)
        v_subphase(kc)
    es_v.close()

    es_qa = contextlib.ExitStack()
    load_cast_x(xT, 0)
    proj_sub(0, wq, True)
    drain_prep(NPB)
    assert prep_state[1] == NPB
    es_p.close()

    QSAMP = min(512, TLOC)
    es_ks = contextlib.ExitStack()
    ksamp_sb = SB("ksamp_sb", [128, NH * 128], BF16, "left", es_ks)
    qsmp_sb = [SB(f"qsmp{i}", [128, QSAMP], BF16, "left", es_ks)
               for i in range(2)]
    b_ksamp = Buf()
    b_qsmp = [Buf(), Buf()]
    c.emit_dma(lambda: SY.dma_start(
        out=ksamp_sb.ap().rearrange("p (h k) -> p h k", k=128),
        in_=K_loc[:, 0:128].rearrange("(h p) k -> p h k", p=128)),
        SY, "sy", "ksmp", reads=(b_kloc[0], b_kloc[1]), writes=(b_ksamp,))
    with nc.psum_tensor("smp0", [128, 512], F32) as Smp0, \
         nc.psum_tensor("smp1", [128, 512], F32) as Smp1:
        SMP = [Smp0, Smp1]
        b_smp = [Buf(), Buf()]
        for h in range(NH):
            sp = h % 2
            c.emit_dma(lambda h=h, sp=sp: SY.dma_start(
                out=qsmp_sb[sp].ap(),
                in_=q_loc[h * 128:(h + 1) * 128, 0:QSAMP]),
                SY, "sy", f"qsm{sp}",
                reads=(b_qloc[0], b_qloc[1]), writes=(b_qsmp[sp],))
            c.emit(lambda h=h, sp=sp: TE.matmul(
                SMP[sp][0:1, 0:QSAMP], ksamp_sb[:, h * 128:h * 128 + 1],
                qsmp_sb[sp].ap(), start=True, stop=True),
                TE, "te", "pe",
                reads=(b_ksamp, b_qsmp[sp]),
                writes=((b_smp[sp], c.psg) if h == 0 else (b_smp[sp],)))
            c.emit(lambda sp=sp: VE.reduce_max(sc1_sb.ap(),
                                               SMP[sp][0:1, 0:QSAMP],
                                               axis=AXX),
                   VE, "ve", "dv", reads=(b_smp[sp],), writes=(b_sc1,))
            if h == 0:
                c.emit(lambda: VE.tensor_copy(sc2_sb.ap(), sc1_sb.ap()),
                       VE, "ve", "dv", reads=(b_sc1,), writes=(b_sc2,))
            else:
                c.emit(lambda: VE.tensor_max(sc2_sb.ap(), sc2_sb.ap(),
                                             sc1_sb.ap()),
                       VE, "ve", "dv", reads=(b_sc1,), writes=(b_sc2,))
        c.psg = c.snap_guard()
    c.emit(lambda: VE.tensor_scalar(sc1_sb.ap(), sc2_sb.ap(), cfg.S_SCALE,
                                    C_MARGIN, ALU.mult, ALU.add),
           VE, "ve", "dv", reads=(b_sc2,), writes=(b_sc1,))
    c.emit(lambda: VE.tensor_scalar(sc1_sb.ap(), sc1_sb.ap(), -1.0, None,
                                    ALU.mult),
           VE, "ve", "dv", writes=(b_sc1,))
    with nc.psum_tensor("ncp", [128, 512], F32) as NCP:
        b_ncp = Buf()
        c.emit(lambda: TE.matmul(NCP[:, 0:1], ones_row_f.ap(), sc1_sb.ap(),
                                 start=True, stop=True),
               TE, "te", "pe", reads=(b_sc1, b_ones), writes=(b_ncp, c.psg))
        c.emit(lambda: VE.tensor_copy(negc_sb.ap(), NCP[:, 0:1]),
               VE, "ve", "dv", reads=(b_ncp,), writes=(b_negc,))
        c.psg = c.snap_guard()

    c.barrier_all(ENGS)
    es_ks.close()

    es_rmain.close()
    attn_sb = SB("attn_sb", [128, NH * TLOC], BF16, "right")
    Q_buf = [SB(f"Q_buf{i}", [128, TLOC], BF16, "right", es_qa)
             for i in range(2)]
    K_buf = [SB(f"K_buf{i}", [128, TLOC], BF16, "right", es_qa)
             for i in range(3)]
    V_buf = [SB(f"V_buf{i}", [128, NTC * 128], BF16, "right", es_qa)
             for i in range(3)]
    P_sb = [SB(f"P_sb{i}", [128, QHS], BF16, "right", es_qa)
            for i in range(3)]
    racc = SB("racc", [128, 2 * TLOC], F32, "right", es_qa)
    rb_sb = [SB(f"rb_sb{i}", [128, QHS], F32, "right", es_qa)
             for i in range(2)]
    recip_sb = [SB(f"recip_sb{i}", [1, QHS], F32, "right", es_qa)
                for i in range(2)]

    b_qbuf = [Buf(), Buf()]
    b_kbuf = [Buf(), Buf(), Buf()]
    b_vbuf = [Buf(), Buf(), Buf()]
    b_psb = [Buf(), Buf(), Buf()]
    b_racc = [Buf(), Buf(), Buf(), Buf()]
    b_rb = [Buf(), Buf()]
    b_recip = [Buf(), Buf()]
    b_attn = Buf()

    v_view = V_loc.rearrange("(r tc p) e -> r p tc e", tc=NTC, p=128)

    with nc.psum_tensor("S_ps0", [128, max(512, QHS)], F32) as Sp0, \
         nc.psum_tensor("S_ps1", [128, max(512, QHS)], F32) as Sp1, \
         nc.psum_tensor("O_ps0", [128, max(512, QHS)], F32) as Op0, \
         nc.psum_tensor("O_ps1", [128, max(512, QHS)], F32) as Op1:
        S_ps = [Sp0, Sp1]
        O_ps = [Op0, Op1]
        b_sps = [Buf(), Buf()]
        b_ops = [Buf(), Buf()]
        NBLK = NH * RG

        def emit_kv_load(blk):
            h, r = blk // RG, blk % RG
            pr = blk % 3
            c.emit_dma(lambda h=h, r=r, pr=pr: SY.dma_start(
                out=K_buf[pr].ap(),
                in_=K_loc[h * 128:(h + 1) * 128, r * TLOC:(r + 1) * TLOC]),
                SY, "sy", f"kb{pr}", reads=(b_kloc[0], b_kloc[1]),
                writes=(b_kbuf[pr],))
            c.emit_dma(lambda h=h, r=r, pr=pr: SY.dma_start(
                out=V_buf[pr].ap().rearrange("p (tc e) -> p tc e", e=128),
                in_=v_view[r, :, :, h * 128:(h + 1) * 128]),
                SY, "sy", f"vb{pr}", reads=(b_vloc[0], b_vloc[1]),
                writes=(b_vbuf[pr],))

        def emit_q_load(h):
            qp = h % 2
            c.emit_dma(lambda h=h, qp=qp: SY.dma_start(
                out=Q_buf[qp].ap(), in_=q_loc[h * 128:(h + 1) * 128, :]),
                SY, "sy", f"qb{qp}",
                reads=(b_qloc[0], b_qloc[1]), writes=(b_qbuf[qp],))

        emit_kv_load(0)
        emit_kv_load(1)
        emit_kv_load(2)
        emit_q_load(0)
        emit_q_load(1)

        pend = []

        def flush_one():
            if pend:
                pv_fn, racc_fn = pend.pop(0)
                pv_fn()
                racc_fn()

        def flush_pend():
            while pend:
                flush_one()

        u = 0
        for h in range(NH):
            for r in range(RG):
                blk = h * RG + r
                first_of_blk = True
                kb = blk % 3
                for qh in range(2):
                    for tc in range(NTC):
                        j = r * NTC + tc
                        up = u % 3
                        sp2 = u % 2

                        qp = h % 2

                        def mm_s(h=h, tc=tc, qh=qh, sp2=sp2, kb=kb, qp=qp):
                            last = None
                            for (q0, ql) in _sl(QHS):
                                last = TE.matmul(
                                    S_ps[sp2][:, q0:q0 + ql],
                                    K_buf[kb][:, tc * 128:(tc + 1) * 128],
                                    Q_buf[qp][:, qh * QHS + q0:
                                              qh * QHS + q0 + ql],
                                    start=True, stop=True)
                            return last
                        c.emit(mm_s, TE, "te", "pe",
                               reads=(b_kbuf[kb], b_qbuf[qp]),
                               writes=((b_sps[sp2], c.psg) if u == 0
                                       else (b_sps[sp2],)))
                        c.emit(lambda up=up, sp2=sp2: SC.activation(
                            P_sb[up].ap(), S_ps[sp2][:, 0:QHS], AF.Exp,
                            bias=negc_sb.ap(), scale=cfg.S_SCALE),
                            SC, "sc", "ac", reads=(b_sps[sp2], b_negc),
                            writes=(b_psb[up],))
                        if len(pend) >= 2:
                            flush_one()
                        if first_of_blk:
                            first_of_blk = False
                            if 3 <= blk + 2 < NBLK + 1 and blk + 2 < NBLK:
                                flush_pend()
                                emit_kv_load(blk + 2)

                        def pv_fn(j=j, tc=tc, qh=qh, up=up, kb=kb):
                            def mm_pv():
                                last = None
                                for (q0, ql) in _sl(QHS):
                                    last = TE.matmul(
                                        O_ps[qh][:, q0:q0 + ql],
                                        V_buf[kb][:, tc * 128:(tc + 1) * 128],
                                        P_sb[up][:, q0:q0 + ql],
                                        start=(j == 0),
                                        stop=(j == RG * NTC - 1))
                                return last
                            c.emit(mm_pv, TE, "te", "pe",
                                   reads=(b_vbuf[kb], b_psb[up]),
                                   writes=(b_ops[qh],))

                        if (u % 4) != 3:
                            eng, ekey, sem = VE, "ve", "dv"
                        else:
                            eng, ekey, sem = GP, "pl", "pl"

                        ai = qh * 2 + (j % 2)

                        def racc_fn(eng=eng, ekey=ekey, sem=sem, j=j, ai=ai,
                                    up=up):
                            sl_ = racc[:, ai * QHS:(ai + 1) * QHS]
                            if j < 2:
                                c.emit(lambda: eng.tensor_copy(
                                    sl_, P_sb[up].ap()),
                                    eng, ekey, sem, reads=(b_psb[up],),
                                    writes=(b_racc[ai],))
                            else:
                                c.emit(lambda: eng.tensor_add(
                                    sl_, sl_, P_sb[up].ap()),
                                    eng, ekey, sem, reads=(b_psb[up],),
                                    writes=(b_racc[ai],))

                        pend.append((pv_fn, racc_fn))
                        u += 1
            flush_pend()
            if h + 2 < NH:
                emit_q_load(h + 2)

            def st_rs(qh, h=h):
                sp = qh

                def mm_rs():
                    last = None
                    for (q0, ql) in _sl(QHS):
                        for pi in range(2):
                            ai = qh * 2 + pi
                            last = TE.matmul(S_ps[sp][0:1, q0:q0 + ql],
                                             ones_col_f.ap(),
                                             racc[:, ai * QHS + q0:
                                                  ai * QHS + q0 + ql],
                                             start=(pi == 0), stop=(pi == 1))
                    return last
                c.emit(mm_rs, TE, "te", "pe",
                       reads=(b_racc[qh * 2], b_racc[qh * 2 + 1], b_ones),
                       writes=(b_sps[sp],))

            def st_cb(qh, h=h):
                sp = qh
                c.emit(lambda: SC.activation(
                    recip_sb[qh].ap(), S_ps[sp][0:1, 0:QHS], AF.Ln,
                    bias=zer_sb.ap()),
                    SC, "sc", "ac", reads=(b_sps[sp], b_ones),
                    writes=(b_recip[qh],))
                c.emit(lambda: SC.activation(
                    recip_sb[qh].ap(), recip_sb[qh].ap(), AF.Exp,
                    scale=-1.0),
                    SC, "sc", "ac", reads=(b_recip[qh],),
                    writes=(b_recip[qh],))

                def mm_bc():
                    last = None
                    for (q0, ql) in _sl(QHS):
                        last = TE.matmul(S_ps[sp][:, q0:q0 + ql],
                                         ones_row_f.ap(),
                                         recip_sb[qh][0:1, q0:q0 + ql],
                                         start=True, stop=True)
                    return last
                c.emit(mm_bc, TE, "te", "pe", reads=(b_recip[qh], b_ones),
                       writes=(b_sps[sp],))

            def st_rm(qh, h=h):
                sp = qh
                c.emit(lambda: VE.tensor_copy(
                    rb_sb[qh].ap(), S_ps[sp][:, 0:QHS]),
                    VE, "ve", "dv", reads=(b_sps[sp],), writes=(b_rb[qh],))
                dst = attn_sb[:, h * TLOC + qh * QHS:
                              h * TLOC + (qh + 1) * QHS]
                c.emit(lambda dst=dst: VE.tensor_mul(
                    dst, O_ps[qh][:, 0:QHS], rb_sb[qh].ap()),
                    VE, "ve", "dv", reads=(b_ops[qh], b_rb[qh]),
                    writes=(b_attn,))

            st_rs(0)
            st_rs(1)
            st_cb(0)
            st_cb(1)
            st_rm(0)
            st_rm(1)
        c.psg = c.snap_guard()

    es_qa.close()
    c.barrier_all(ENGS)

    x2_sb2 = [SB(f"x2_sb{i}", [128, DC * TTS], F32, "left")
              for i in range(2)]
    x2q_sb = SB("x2q_sb", [128, DC * TTS], BF16, "left")
    acc_sb = SB("acc_sb", [128, TTS], F32, "left")
    acc2_sb = SB("acc2_sb", [128, TTS], F32, "left")
    sqt_sb = SB("sqt_sb", [128, TTS], F32, "left")
    qt1_sb = SB("qt1_sb", [128, TTS], BF16, "left")
    qt2_sb = SB("qt2_sb", [128, TTS], BF16, "left")
    xres = [SB(f"xres{i}", [128, TTS], F32, "left") for i in range(2)]
    ab_sb = SB("ab_sb", [128, TTS], F32, "left")
    bb_sb = SB("bb_sb", [128, TTS], F32, "left")
    m_sb = SB("m_sb", [1, TTS], F32, "left")
    v_sb = SB("v_sb", [1, TTS], F32, "left")
    r_sb = SB("r_sb", [1, TTS], F32, "left")
    bsm_sb = SB("bsm_sb", [1, TTS], F32, "left")

    b_x22 = [[Buf() for _ in range(DC)] for _ in range(2)]
    b_x2q = Buf()
    b_acc, b_acc2, b_sqt = Buf(), Buf(), Buf()
    b_qt1, b_qt2 = Buf(), Buf()
    b_xres = [Buf(), Buf()]
    b_ab, b_bb = Buf(), Buf()
    b_m, b_v, b_r, b_bsm = Buf(), Buf(), Buf(), Buf()
    b_outb = [Buf(), Buf()]

    with nc.psum_tensor("M_ps0", [128, max(512, TTS)], F32) as Mp0, \
         nc.psum_tensor("M_ps1", [128, max(512, TTS)], F32) as Mp1, \
         nc.psum_tensor("L_ps0", [128, max(512, TTS)], F32) as Lp0, \
         nc.psum_tensor("L_ps1", [128, max(512, TTS)], F32) as Lp1:
        M_ps = [Mp0, Mp1]
        L_ps = [Lp0, Lp1]
        b_mps = [Buf(), Buf()]
        b_lps = [Buf(), Buf()]

        def ln_finalize():
            def mm_sum():
                last = None
                for (t0, tl) in _sl(TTS):
                    last = TE.matmul(L_ps[0][0:1, t0:t0 + tl],
                                     ones_col_f.ap(), acc_sb[:, t0:t0 + tl],
                                     start=True, stop=True)
                return last

            def mm_sum2():
                last = None
                for (t0, tl) in _sl(TTS):
                    last = TE.matmul(L_ps[1][0:1, t0:t0 + tl],
                                     ones_col_f.ap(), acc2_sb[:, t0:t0 + tl],
                                     start=True, stop=True)
                return last
            c.emit(mm_sum, TE, "te", "pe", reads=(b_acc, b_ones),
                   writes=(b_lps[0],))
            c.emit(mm_sum2, TE, "te", "pe", reads=(b_acc2, b_ones),
                   writes=(b_lps[1],))
            c.emit(lambda: VE.tensor_scalar(m_sb.ap(), L_ps[0][0:1, 0:TTS],
                                            1.0 / D, None, ALU.mult),
                   VE, "ve", "dv", reads=(b_lps[0],), writes=(b_m,))
            c.emit(lambda: VE.tensor_scalar(v_sb.ap(), L_ps[1][0:1, 0:TTS],
                                            1.0 / D, None, ALU.mult),
                   VE, "ve", "dv", reads=(b_lps[1],), writes=(b_v,))
            c.emit(lambda: VE.tensor_mul(bsm_sb.ap(), m_sb.ap(), m_sb.ap()),
                   VE, "ve", "dv", reads=(b_m,), writes=(b_bsm,))
            c.emit(lambda: VE.tensor_sub(v_sb.ap(), v_sb.ap(), bsm_sb.ap()),
                   VE, "ve", "dv", reads=(b_bsm,), writes=(b_v,))
            c.emit(lambda: SC.activation(r_sb.ap(), v_sb.ap(), AF.Ln,
                                         bias=eps_sb.ap()),
                   SC, "sc", "ac", reads=(b_v, b_ones), writes=(b_r,))
            c.emit(lambda: SC.activation(r_sb.ap(), r_sb.ap(), AF.Exp,
                                         scale=-0.5),
                   SC, "sc", "ac", reads=(b_r,), writes=(b_r,))
            c.emit(lambda: VE.scalar_tensor_tensor(
                bsm_sb.ap(), m_sb.ap(), -1.0, r_sb.ap(), ALU.mult, ALU.mult),
                VE, "ve", "dv", reads=(b_m, b_r), writes=(b_bsm,))

            def mm_ba():
                last = None
                for (t0, tl) in _sl(TTS):
                    last = TE.matmul(L_ps[0][:, t0:t0 + tl], ones_row_f.ap(),
                                     r_sb[0:1, t0:t0 + tl],
                                     start=True, stop=True)
                return last

            def mm_bb():
                last = None
                for (t0, tl) in _sl(TTS):
                    last = TE.matmul(L_ps[1][:, t0:t0 + tl], ones_row_f.ap(),
                                     bsm_sb[0:1, t0:t0 + tl],
                                     start=True, stop=True)
                return last
            c.emit(mm_ba, TE, "te", "pe", reads=(b_r, b_ones),
                   writes=(b_lps[0],))
            c.emit(mm_bb, TE, "te", "pe", reads=(b_bsm, b_ones),
                   writes=(b_lps[1],))
            c.emit(lambda: SC.activation(ab_sb.ap(), L_ps[0][:, 0:TTS],
                                         AF.Copy),
                   SC, "sc", "ac", reads=(b_lps[0],), writes=(b_ab,))
            c.emit(lambda: SC.activation(bb_sb.ap(), L_ps[1][:, 0:TTS],
                                         AF.Copy),
                   SC, "sc", "ac", reads=(b_lps[1],), writes=(b_bb,))

        for tt in range(NTT):
            x2_sb = x2_sb2[tt % 2]
            b_x2 = b_x22[tt % 2]
            for ec in range(DC):
                pr = ec % 2
                ps3 = ec % 3
                c.emit_dma(lambda ec=ec, ps3=ps3: SY.dma_start(
                    out=wstage[ps3].ap(), in_=wo[:, ec * D:(ec + 1) * D]),
                    SY, "sy", f"wst{ps3}", writes=(b_wstage[ps3],))
                c.emit(lambda pr=pr, ps3=ps3: SC.activation(
                    wbf[pr].ap(), wstage[ps3].ap(), AF.Copy),
                    SC, "sc", "ac", reads=(b_wstage[ps3],),
                    writes=(b_wbf[pr],))
                c.emit_dma(lambda ec=ec, tt=tt, pr=pr: SC.dma_start(
                    out=xres[pr].ap(),
                    in_=xT[ec * 128:(ec + 1) * 128,
                           tt * TTS:(tt + 1) * TTS]),
                    SC, "sc", f"xrs{pr}", writes=(b_xres[pr],))

                def mm(ec=ec, tt=tt, pr=pr):
                    last = None
                    for dc in range(DC):
                        lhsT = wbf[pr][:, dc * 128:(dc + 1) * 128]
                        for (t0, tl) in _sl(TTS):
                            last = TE.matmul(
                                M_ps[pr][:, t0:t0 + tl], lhsT,
                                attn_sb[:, dc * TLOC + tt * TTS + t0:
                                        dc * TLOC + tt * TTS + t0 + tl],
                                start=(dc == 0), stop=(dc == DC - 1))
                    return last
                c.emit(mm, TE, "te", "pe", reads=(b_wbf[pr], b_attn),
                       writes=((b_mps[pr], c.psg) if tt == 0 and ec == 0
                               else (b_mps[pr],)))
                dst = x2_sb[:, ec * TTS:(ec + 1) * TTS]
                c.emit(lambda pr=pr, dst=dst: VE.tensor_add(
                    dst, M_ps[pr][:, 0:TTS], xres[pr].ap()),
                    VE, "ve", "dv", reads=(b_mps[pr], b_xres[pr]),
                    writes=(b_x2[ec],))
                if ec == 0:
                    c.emit(lambda dst=dst: VE.tensor_copy(acc_sb.ap(), dst),
                           VE, "ve", "dv", reads=(b_x2[ec],), writes=(b_acc,))
                else:
                    c.emit(lambda dst=dst: VE.tensor_add(acc_sb.ap(),
                                                         acc_sb.ap(), dst),
                           VE, "ve", "dv", reads=(b_x2[ec],), writes=(b_acc,))
                c.emit(lambda dst=dst: VE.tensor_mul(sqt_sb.ap(), dst, dst),
                       VE, "ve", "dv", reads=(b_x2[ec],), writes=(b_sqt,))
                if ec == 0:
                    c.emit(lambda: VE.tensor_copy(acc2_sb.ap(), sqt_sb.ap()),
                           VE, "ve", "dv", reads=(b_sqt,), writes=(b_acc2,))
                else:
                    c.emit(lambda: VE.tensor_add(acc2_sb.ap(), acc2_sb.ap(),
                                                 sqt_sb.ap()),
                           VE, "ve", "dv", reads=(b_sqt,), writes=(b_acc2,))
            ln_finalize()
            for ec in range(DC):
                dst = x2_sb[:, ec * TTS:(ec + 1) * TTS]
                c.emit(lambda dst=dst: VE.tensor_mul(dst, dst, ab_sb.ap()),
                       VE, "ve", "dv", reads=(b_ab,), writes=(b_x2[ec],))
                c.emit(lambda dst=dst: VE.tensor_add(dst, dst, bb_sb.ap()),
                       VE, "ve", "dv", reads=(b_bb,), writes=(b_x2[ec],))
                c.emit(lambda dst=dst: SC.activation(
                    qt1_sb.ap(), dst, AF.Sign, bias=xthn_sb.ap()),
                    SC, "sc", "ac", reads=(b_x2[ec], b_ones), writes=(b_qt1,))
                c.emit(lambda dst=dst: SC.activation(
                    qt2_sb.ap(), dst, AF.Sign, bias=xthp_sb.ap()),
                    SC, "sc", "ac", reads=(b_x2[ec], b_ones), writes=(b_qt2,))
                c.emit(lambda: VE.tensor_add(qt1_sb.ap(), qt1_sb.ap(),
                                             qt2_sb.ap()),
                       VE, "ve", "dv", reads=(b_qt2,), writes=(b_qt1,))
                qdst = x2q_sb[:, ec * TTS:(ec + 1) * TTS]
                c.emit(lambda qdst=qdst: SC.activation(
                    qdst, qt1_sb.ap(), AF.Copy, scale=SC_X / 2.0),
                    SC, "sc", "ac", reads=(b_qt1,), writes=(b_x2q,))

            for oc in range(DC):
                pr = oc % 2
                c.emit_dma(lambda oc=oc, pr=pr: SY.dma_start(
                    out=wbf[pr].ap(), in_=wr_loc[:, oc * D:(oc + 1) * D]),
                    SY, "sy", f"wrb{pr}", reads=(b_wrloc,),
                    writes=(b_wbf[pr],))

                def mm(oc=oc, pr=pr):
                    last = None
                    for dc in range(DC):
                        lhsT = wbf[pr][:, dc * 128:(dc + 1) * 128]
                        for (t0, tl) in _sl(TTS):
                            last = TE.matmul(
                                M_ps[pr][:, t0:t0 + tl], lhsT,
                                x2q_sb[:, dc * TTS + t0: dc * TTS + t0 + tl],
                                start=(dc == 0), stop=(dc == DC - 1))
                    return last
                c.emit(mm, TE, "te", "pe", reads=(b_wbf[pr], b_x2q),
                       writes=(b_mps[pr],))
                dst = x2_sb[:, oc * TTS:(oc + 1) * TTS]
                c.emit(lambda pr=pr, dst=dst: VE.tensor_add(
                    dst, M_ps[pr][:, 0:TTS], dst),
                    VE, "ve", "dv", reads=(b_mps[pr],), writes=(b_x2[oc],))
                if oc == 0:
                    c.emit(lambda dst=dst: VE.tensor_copy(acc_sb.ap(), dst),
                           VE, "ve", "dv", reads=(b_x2[oc],), writes=(b_acc,))
                else:
                    c.emit(lambda dst=dst: VE.tensor_add(acc_sb.ap(),
                                                         acc_sb.ap(), dst),
                           VE, "ve", "dv", reads=(b_x2[oc],), writes=(b_acc,))
                c.emit(lambda dst=dst: VE.tensor_mul(sqt_sb.ap(), dst, dst),
                       VE, "ve", "dv", reads=(b_x2[oc],), writes=(b_sqt,))
                if oc == 0:
                    c.emit(lambda: VE.tensor_copy(acc2_sb.ap(), sqt_sb.ap()),
                           VE, "ve", "dv", reads=(b_sqt,), writes=(b_acc2,))
                else:
                    c.emit(lambda: VE.tensor_add(acc2_sb.ap(), acc2_sb.ap(),
                                                 sqt_sb.ap()),
                           VE, "ve", "dv", reads=(b_sqt,), writes=(b_acc2,))
            ln_finalize()
            for oc in range(DC):
                dst = x2_sb[:, oc * TTS:(oc + 1) * TTS]
                c.emit(lambda dst=dst: VE.tensor_mul(dst, dst, ab_sb.ap()),
                       VE, "ve", "dv", reads=(b_ab,), writes=(b_x2[oc],))
                c.emit(lambda dst=dst: VE.tensor_add(dst, dst, bb_sb.ap()),
                       VE, "ve", "dv", reads=(b_bb,), writes=(b_x2[oc],))
                c.emit_dma(lambda oc=oc, tt=tt, dst=dst: GP.dma_start(
                    out=outT[oc * 128:(oc + 1) * 128,
                             tt * TTS:(tt + 1) * TTS],
                    in_=dst),
                    GP, "gq", f"out{oc % 2}", reads=(b_x2[oc],),
                    writes=(b_outb[oc % 2],))

    c.barrier_all(ENGS)
    es.close()
    return nc


def _reorder_hblock(WT, DC):
    d, e = WT.shape
    nh = e // 128
    a = WT.reshape(DC, 128, nh, 128).transpose(1, 2, 0, 3)
    return np.ascontiguousarray(a.reshape(128, nh * DC * 128))


def _reorder_plain(WT, DC):
    d, e = WT.shape
    a = WT.reshape(DC, 128, e).transpose(1, 0, 2)
    return np.ascontiguousarray(a.reshape(128, DC * e))


def kernel(L0_attn_x, L0_attn_Wq, L0_attn_Wk, L0_attn_Wv, L0_attn_Wo,
           L0_ffn_weight, L0_ffn_rram_variation):
    cfg = Cfg(NH=16, TLOC=2048, RG=4)
    x = np.asarray(L0_attn_x, dtype=np.float32)
    B, T, Dm = x.shape
    f32 = lambda a: np.asarray(a, dtype=np.float32)
    ins = {
        "wq": _reorder_hblock(f32(L0_attn_Wq).T, cfg.DC),
        "wk": _reorder_hblock(f32(L0_attn_Wk).T, cfg.DC),
        "wv": _reorder_plain(f32(L0_attn_Wv).T, cfg.DC),
        "wo": _reorder_hblock(f32(L0_attn_Wo).T, cfg.DC),
        "wf": _reorder_hblock(f32(L0_ffn_weight).T, cfg.DC),
        "wr": _reorder_hblock(f32(L0_ffn_rram_variation).T, cfg.DC),
    }
    nc = build(cfg)
    xkTs = [np.ascontiguousarray(x[b].T) for b in range(B)]
    in_maps = []
    ncores = 2 * cfg.RG
    for core in range(ncores):
        b, sl = core // cfg.RG, core % cfg.RG
        m = dict(ins)
        m["xT"] = np.ascontiguousarray(
            x[b, sl * cfg.TLOC:(sl + 1) * cfg.TLOC, :].T)
        m["xkT"] = xkTs[b]
        in_maps.append(m)
    res = run_bass_kernel_spmd(nc, in_maps, core_ids=list(range(ncores)))
    out = np.empty((B, T, Dm), dtype=np.float32)
    for core in range(ncores):
        b, sl = core // cfg.RG, core % cfg.RG
        out[b, sl * cfg.TLOC:(sl + 1) * cfg.TLOC, :] = \
            res.results[core]["outT"].T
    return out


# version 42 (final)
# speedup vs baseline: 1.0180x; 1.0092x over previous
import math
import contextlib
import numpy as np
import ml_dtypes
import concourse.bass as bass
import concourse.mybir as mybir
from concourse.bass_utils import run_bass_kernel_spmd

F32 = mybir.dt.float32
BF16 = mybir.dt.bfloat16
AF = mybir.ActivationFunctionType
ALU = mybir.AluOpType
AXX = mybir.AxisListType.X

TH_X, SC_X = 0.33, 0.5
TH_W, SC_W = 0.2, 0.6
LN_EPS = 1e-5
C_MARGIN = 10.0

COMPUTE_EKEYS = ("te",)


class Cfg:
    def __init__(self, NH=16, TLOC=2048, RG=4):
        self.NH = NH
        self.DC = NH
        self.D = NH * 128
        self.TLOC = TLOC
        self.RG = RG
        self.TSEQ = RG * TLOC
        self.NTC = TLOC // 128
        self.QHS = TLOC // 2
        self.TTS = min(512, TLOC // 2)
        self.NTT = TLOC // self.TTS
        self.S_SCALE = 1.0 / math.sqrt(128.0)
        self.PBS = min(256, self.DC * self.D)
        self.NPB = (self.DC * self.D) // self.PBS


def _sl(total, step=512):
    o, out = 0, []
    while o < total:
        s = min(step, total - o)
        out.append((o, s))
        o += s
    return out


class Buf:
    __slots__ = ("w", "r")

    def __init__(self):
        self.w = None
        self.r = {}


class Ctx:
    def __init__(self, nc):
        self.nc = nc
        self.sem = {}
        self.cnt = {}
        for name in ("pe", "ac", "dv", "pl"):
            self.sem[name] = nc.alloc_semaphore(name="sem_" + name)
            self.cnt[name] = 0
        self.waited = {}

    def dsem(self, key):
        if key not in self.sem:
            self.sem[key] = self.nc.alloc_semaphore(name="sem_" + key)
            self.cnt[key] = 0
        return key

    def inc(self, instr, s, n=1):
        instr.then_inc(self.sem[s], n)
        self.cnt[s] += n
        return self.cnt[s]

    def wait(self, eng, ekey, s, val):
        if val is None or val <= 0:
            return
        k = (ekey, s)
        if self.waited.get(k, 0) >= val:
            return
        eng.wait_ge(self.sem[s], val)
        self.waited[k] = val

    def _wait_tok(self, eng, qkey, tok):
        tekey, sem, val = tok
        if tekey == qkey and tekey in COMPUTE_EKEYS:
            return
        self.wait(eng, qkey, sem, val)

    def emit(self, fn, eng, ekey, sem, reads=(), writes=(), dn=1, qkey=None):
        qkey = qkey or ekey
        for b in reads:
            if b.w is not None:
                self._wait_tok(eng, qkey, b.w)
        for b in writes:
            if b.w is not None:
                self._wait_tok(eng, qkey, b.w)
            for t in list(b.r.values()):
                self._wait_tok(eng, qkey, t)
        instr = fn()
        tok = self.inc(instr, sem, dn)
        t = (ekey, sem, tok)
        for b in reads:
            b.r[ekey] = t
        for b in writes:
            b.w = t
            b.r = {}
        return tok

    def emit_dma(self, fn, eng, qkey, key, reads=(), writes=()):
        self.dsem(key)
        return self.emit(fn, eng, "dq_" + key, key, reads=reads,
                         writes=writes, dn=16, qkey=qkey)

    def barrier_all(self, engs):
        snap = dict(self.cnt)
        for eng, ekey in engs:
            for s in snap:
                self.wait(eng, ekey, s, snap[s])

    def snap_guard(self):
        g = Buf()
        for s in ("pe", "ac", "dv", "pl"):
            g.r["snap_" + s] = ("snap_" + s, s, self.cnt[s])
        return g

    @staticmethod
    def guard(*bufs):
        g = Buf()
        i = 0
        for b in bufs:
            if b.w is not None:
                g.r[f"g{i}"] = b.w
                i += 1
            for t in b.r.values():
                g.r[f"g{i}"] = t
                i += 1
        return g


def build(cfg: Cfg):
    nc = bass.Bass("TRN2", target_bir_lowering=False)
    NH, DC, D, TLOC = cfg.NH, cfg.DC, cfg.D, cfg.TLOC
    RG, NTC, QHS, TTS, NTT = cfg.RG, cfg.NTC, cfg.QHS, cfg.TTS, cfg.NTT
    PBS, NPB, TSEQ = cfg.PBS, cfg.NPB, cfg.TSEQ
    EVW = max(D, TLOC)
    PJW = min(TLOC, 1024)

    xT = nc.dram_tensor("xT", [D, TLOC], F32, kind="ExternalInput")
    xkT = nc.dram_tensor("xkT", [D, TSEQ], F32, kind="ExternalInput")
    wq = nc.dram_tensor("wq", [128, DC * D], F32, kind="ExternalInput")
    wk = nc.dram_tensor("wk", [128, DC * D], F32, kind="ExternalInput")
    wv = nc.dram_tensor("wv", [128, DC * D], F32, kind="ExternalInput")
    wo = nc.dram_tensor("wo", [128, DC * D], F32, kind="ExternalInput")
    wf = nc.dram_tensor("wf", [128, DC * D], F32, kind="ExternalInput")
    wr = nc.dram_tensor("wr", [128, DC * D], F32, kind="ExternalInput")
    outT = nc.dram_tensor("outT", [D, TLOC], F32, kind="ExternalOutput")

    K_loc = nc.dram_tensor("K_loc", [D, TSEQ], BF16)
    q_loc = nc.dram_tensor("q_loc", [D, TLOC], BF16)
    V_loc = nc.dram_tensor("V_loc", [TSEQ, D], BF16)
    wr_loc = nc.dram_tensor("wr_loc", [128, DC * D], BF16)

    TE, SC, VE, GP, SY = nc.tensor, nc.scalar, nc.vector, nc.gpsimd, nc.sync
    c = Ctx(nc)
    c.psg = Buf()
    ENGS = [(TE, "te"), (SC, "sc"), (VE, "ve"), (GP, "pl"), (SY, "sy")]
    es = contextlib.ExitStack()

    def SB(name, shape, dtype, side, stack=None):
        st = stack if stack is not None else es
        return st.enter_context(nc.sbuf_tensor(name, shape, dtype, side=side))

    wstage = [SB(f"wstage{i}", [128, D], F32, "left") for i in range(3)]
    wbf = [SB(f"wbf{i}", [128, D], BF16, "left") for i in range(2)]
    ones_col_f = SB("ones_col_f", [128, 1], F32, "left")
    ones_col_b = SB("ones_col_b", [128, 1], BF16, "left")
    ones_row_f = SB("ones_row_f", [1, 128], F32, "left")
    negc_sb = SB("negc_sb", [128, 1], F32, "left")
    sc1_sb = SB("sc1_sb", [1, 1], F32, "left")
    sc2_sb = SB("sc2_sb", [1, 1], F32, "left")
    eps_sb = SB("eps_sb", [1, 1], F32, "left")
    thn_sb = SB("thn_sb", [128, 1], F32, "left")
    thp_sb = SB("thp_sb", [128, 1], F32, "left")
    xthn_sb = SB("xthn_sb", [128, 1], F32, "left")
    xthp_sb = SB("xthp_sb", [128, 1], F32, "left")
    zer_sb = SB("zer_sb", [1, 1], F32, "left")

    b_wstage = [Buf(), Buf(), Buf()]
    b_wbf = [Buf(), Buf()]
    b_negc = Buf()
    b_sc1, b_sc2 = Buf(), Buf()
    b_ones = Buf()

    c.emit(lambda: VE.memset(ones_col_f.ap(), 1.0), VE, "ve", "dv",
           writes=(b_ones,))
    c.emit(lambda: VE.memset(ones_col_b.ap(), 1.0), VE, "ve", "dv",
           writes=(b_ones,))
    c.emit(lambda: VE.memset(ones_row_f.ap(), 1.0), VE, "ve", "dv",
           writes=(b_ones,))
    c.emit(lambda: VE.memset(eps_sb.ap(), LN_EPS), VE, "ve", "dv",
           writes=(b_ones,))
    c.emit(lambda: VE.memset(thn_sb.ap(), -TH_W), VE, "ve", "dv",
           writes=(b_ones,))
    c.emit(lambda: VE.memset(thp_sb.ap(), TH_W), VE, "ve", "dv",
           writes=(b_ones,))
    c.emit(lambda: VE.memset(xthn_sb.ap(), -TH_X), VE, "ve", "dv",
           writes=(b_ones,))
    c.emit(lambda: VE.memset(xthp_sb.ap(), TH_X), VE, "ve", "dv",
           writes=(b_ones,))
    c.emit(lambda: VE.memset(zer_sb.ap(), 0.0), VE, "ve", "dv",
           writes=(b_ones,))

    es_rmain = contextlib.ExitStack()
    xbf = SB("xbf", [128, DC * TLOC], BF16, "right", es_rmain)
    evac = [SB(f"evac{i}", [128, EVW], BF16, "right", es_rmain)
            for i in range(2)]
    xstage = [SB(f"xstage{i}", [128, TLOC], F32, "right", es_rmain)
              for i in range(3)]
    b_xbf = Buf()
    b_evac = [Buf(), Buf()]
    b_xstage = [Buf() for _ in range(3)]

    def load_cast_x(src_dram, col0):
        for dc in range(DC):
            pr = dc % 3
            c.emit_dma(lambda dc=dc, pr=pr: SY.dma_start(
                out=xstage[pr].ap(),
                in_=src_dram[dc * 128:(dc + 1) * 128, col0:col0 + TLOC]),
                SY, "sy", f"xst{pr}", writes=(b_xstage[pr],))
            if dc % 2 == 0:
                c.emit(lambda dc=dc, pr=pr: VE.tensor_copy(
                    xbf[:, dc * TLOC:(dc + 1) * TLOC], xstage[pr].ap()),
                    VE, "ve", "dv", reads=(b_xstage[pr],), writes=(b_xbf,))
            else:
                c.emit(lambda dc=dc, pr=pr: SC.activation(
                    xbf[:, dc * TLOC:(dc + 1) * TLOC], xstage[pr].ap(),
                    AF.Copy),
                    SC, "sc", "ac", reads=(b_xstage[pr],), writes=(b_xbf,))

    es_p = contextlib.ExitStack()
    fstage = [SB(f"fstage{i}", [128, PBS], F32, "right", es_p)
              for i in range(4)]
    rstage = [SB(f"rstage{i}", [128, PBS], F32, "right", es_p)
              for i in range(4)]
    ft1 = SB("ft1", [128, PBS], F32, "right", es_p)
    ft2 = SB("ft2", [128, PBS], F32, "right", es_p)
    fout = [SB(f"fout{i}", [128, PBS], BF16, "right", es_p) for i in range(2)]
    b_fstage = [Buf() for _ in range(4)]
    b_rstage = [Buf() for _ in range(4)]
    b_ft1, b_ft2 = Buf(), Buf()
    b_fout = [Buf(), Buf()]
    b_wrloc = Buf()

    def prep_load(b):
        pq = b % 4
        c.emit_dma(lambda b=b, pq=pq: SC.dma_start(
            out=fstage[pq].ap(), in_=wf[:, b * PBS:(b + 1) * PBS]),
            SC, "sc", f"fst{pq}", writes=(b_fstage[pq],))
        c.emit_dma(lambda b=b, pq=pq: SC.dma_start(
            out=rstage[pq].ap(), in_=wr[:, b * PBS:(b + 1) * PBS]),
            SC, "sc", f"rst{pq}", writes=(b_rstage[pq],))

    def prep_compute(b):
        pq = b % 4
        pr = b % 2
        c.emit(lambda pq=pq: SC.activation(ft1.ap(), fstage[pq].ap(),
                                           AF.Sign, bias=thn_sb.ap()),
               SC, "sc", "ac", reads=(b_fstage[pq], b_ones), writes=(b_ft1,))
        c.emit(lambda pq=pq: SC.activation(ft2.ap(), fstage[pq].ap(),
                                           AF.Sign, bias=thp_sb.ap()),
               SC, "sc", "ac", reads=(b_fstage[pq], b_ones), writes=(b_ft2,))
        c.emit(lambda: VE.tensor_add(ft1.ap(), ft1.ap(), ft2.ap()),
               VE, "ve", "dv", reads=(b_ft2,), writes=(b_ft1,))
        c.emit(lambda pq=pq, pr=pr: VE.scalar_tensor_tensor(
            fout[pr].ap(), ft1.ap(), SC_W / 2.0, rstage[pq].ap(),
            ALU.mult, ALU.add),
            VE, "ve", "dv", reads=(b_ft1, b_rstage[pq]),
            writes=(b_fout[pr],))
        c.emit_dma(lambda b=b, pr=pr: GP.dma_start(
            out=wr_loc[:, b * PBS:(b + 1) * PBS], in_=fout[pr].ap()),
            GP, "gq", f"fwr{pr}", reads=(b_fout[pr],), writes=(b_wrloc,))

    prep_state = [0, 0]

    def drain_prep(n):
        for _ in range(n):
            while prep_state[0] < min(NPB, prep_state[1] + 4):
                prep_load(prep_state[0])
                prep_state[0] += 1
            if prep_state[1] < NPB:
                prep_compute(prep_state[1])
                prep_state[1] += 1

    es_v = contextlib.ExitStack()
    wvbf = SB("wvbf", [128, DC * D], BF16, "right", es_v)
    b_wvbf = Buf()
    for b in range(DC):
        pr = b % 3
        c.emit_dma(lambda b=b, pr=pr: SY.dma_start(
            out=wstage[pr].ap(), in_=wv[:, b * D:(b + 1) * D]),
            SY, "sy", f"wst{pr}", writes=(b_wstage[pr],))
        c.emit(lambda b=b, pr=pr: VE.tensor_copy(
            wvbf[:, b * D:(b + 1) * D], wstage[pr].ap()),
            VE, "ve", "dv", reads=(b_wstage[pr],), writes=(b_wvbf,))

    b_vloc = [Buf(), Buf()]
    b_kloc = [Buf(), Buf()]
    b_qloc = [Buf(), Buf()]

    def v_subphase(kc):
        with nc.psum_tensor(f"V{kc}_p0", [128, max(512, D)], F32) as Vp0, \
             nc.psum_tensor(f"V{kc}_p1", [128, max(512, D)], F32) as Vp1:
            V_ps = [Vp0, Vp1]
            b_vps = [Buf(), Buf()]
            for tr in range(NTC):
                pr = tr % 2

                def mm(tr=tr, pr=pr):
                    last = None
                    for dc in range(DC):
                        lhsT = xbf[:, dc * TLOC + tr * 128:
                                   dc * TLOC + (tr + 1) * 128]
                        for (e0, el) in _sl(D):
                            last = TE.matmul(V_ps[pr][:, e0:e0 + el], lhsT,
                                             wvbf[:, dc * D + e0:
                                                  dc * D + e0 + el],
                                             start=(dc == 0),
                                             stop=(dc == DC - 1))
                    return last
                c.emit(mm, TE, "te", "pe", reads=(b_xbf, b_wvbf),
                       writes=((b_vps[pr], c.psg) if tr == 0
                               else (b_vps[pr],)))
                c.emit(lambda pr=pr: SC.activation(evac[pr][:, 0:D],
                                                   V_ps[pr][:, 0:D],
                                                   AF.Copy),
                       SC, "sc", "ac", reads=(b_vps[pr],),
                       writes=(b_evac[pr],))
                row = kc * TLOC + tr * 128
                c.emit_dma(lambda row=row, pr=pr: GP.dma_start(
                    out=V_loc[row:row + 128, :], in_=evac[pr][:, 0:D]),
                    GP, "gq", f"vst{pr}", reads=(b_evac[pr],),
                    writes=(b_vloc[pr],))
            c.psg = c.snap_guard()

    def proj_sub(kc, wdram, is_q, dest_q=None, b_destq=None):
        tag = ("qp" if is_q else f"kp{kc}")
        with nc.psum_tensor(f"{tag}_p0", [128, max(512, PJW)], F32) as Pj0, \
             nc.psum_tensor(f"{tag}_p1", [128, max(512, PJW)], F32) as Pj1:
            P_ps = [Pj0, Pj1]
            b_pps = [Buf(), Buf()]

            def wload(h):
                ps3 = h % 3
                c.emit_dma(lambda h=h, ps3=ps3: SC.dma_start(
                    out=wstage[ps3].ap(), in_=wdram[:, h * D:(h + 1) * D]),
                    SC, "sc", f"wst{ps3}", writes=(b_wstage[ps3],))

            for h in range(min(3, NH)):
                wload(h)
            for h in range(NH):
                pr = h % 2
                ps3 = h % 3
                c.emit(lambda pr=pr, ps3=ps3: VE.tensor_copy(
                    wbf[pr].ap(), wstage[ps3].ap()),
                    VE, "ve", "dv", reads=(b_wstage[ps3],),
                    writes=(b_wbf[pr],))
                if h + 3 < NH:
                    wload(h + 3)
                ev_ap = evac[pr][:, 0:TLOC]
                b_ev = b_evac[pr]
                for thi, (th0, thl) in enumerate(_sl(TLOC, PJW)):
                    pj = (h * 2 + thi) % 2
                    def mm(h=h, pr=pr, pj=pj, th0=th0, thl=thl):
                        last = None
                        for dc in range(DC):
                            lhsT = wbf[pr][:, dc * 128:(dc + 1) * 128]
                            for (t0, tl) in _sl(thl):
                                last = TE.matmul(
                                    P_ps[pj][:, t0:t0 + tl], lhsT,
                                    xbf[:, dc * TLOC + th0 + t0:
                                        dc * TLOC + th0 + t0 + tl],
                                    start=(dc == 0), stop=(dc == DC - 1))
                        return last
                    c.emit(mm, TE, "te", "pe", reads=(b_wbf[pr], b_xbf),
                           writes=((b_pps[pj], c.psg) if h == 0 and th0 == 0
                                   else (b_pps[pj],)))
                    c.emit(lambda pj=pj, ev_ap=ev_ap, th0=th0, thl=thl:
                           SC.activation(ev_ap[:, th0:th0 + thl],
                                         P_ps[pj][:, 0:thl], AF.Copy),
                           SC, "sc", "ac", reads=(b_pps[pj],), writes=(b_ev,))
                if is_q:
                    c.emit_dma(lambda h=h, pr=pr, ev_ap=ev_ap:
                               GP.dma_start(
                        out=q_loc[h * 128:(h + 1) * 128, :], in_=ev_ap),
                        GP, "gq", f"qst{pr}", reads=(b_ev,),
                        writes=(b_qloc[pr],))
                else:
                    c.emit_dma(lambda h=h, kc=kc, pr=pr, ev_ap=ev_ap:
                               GP.dma_start(
                        out=K_loc[h * 128:(h + 1) * 128,
                                  kc * TLOC:(kc + 1) * TLOC], in_=ev_ap),
                        GP, "gq", f"kst{pr}", reads=(b_ev,),
                        writes=(b_kloc[pr],))
                drain_prep(2)
            c.psg = c.snap_guard()

    for kc in range(RG):
        load_cast_x(xkT, kc * TLOC)
        proj_sub(kc, wk, False)
        v_subphase(kc)
    es_v.close()

    es_qa = contextlib.ExitStack()
    load_cast_x(xT, 0)
    proj_sub(0, wq, True)
    drain_prep(NPB)
    assert prep_state[1] == NPB
    es_p.close()

    QSAMP = min(512, TLOC)
    es_ks = contextlib.ExitStack()
    ksamp_sb = SB("ksamp_sb", [128, NH * 128], BF16, "left", es_ks)
    qsmp_sb = [SB(f"qsmp{i}", [128, QSAMP], BF16, "left", es_ks)
               for i in range(2)]
    b_ksamp = Buf()
    b_qsmp = [Buf(), Buf()]
    c.emit_dma(lambda: SY.dma_start(
        out=ksamp_sb.ap().rearrange("p (h k) -> p h k", k=128),
        in_=K_loc[:, 0:128].rearrange("(h p) k -> p h k", p=128)),
        SY, "sy", "ksmp", reads=(b_kloc[0], b_kloc[1]), writes=(b_ksamp,))
    with nc.psum_tensor("smp0", [128, 512], F32) as Smp0, \
         nc.psum_tensor("smp1", [128, 512], F32) as Smp1:
        SMP = [Smp0, Smp1]
        b_smp = [Buf(), Buf()]
        for h in range(NH):
            sp = h % 2
            c.emit_dma(lambda h=h, sp=sp: SY.dma_start(
                out=qsmp_sb[sp].ap(),
                in_=q_loc[h * 128:(h + 1) * 128, 0:QSAMP]),
                SY, "sy", f"qsm{sp}",
                reads=(b_qloc[0], b_qloc[1]), writes=(b_qsmp[sp],))
            c.emit(lambda h=h, sp=sp: TE.matmul(
                SMP[sp][0:1, 0:QSAMP], ksamp_sb[:, h * 128:h * 128 + 1],
                qsmp_sb[sp].ap(), start=True, stop=True),
                TE, "te", "pe",
                reads=(b_ksamp, b_qsmp[sp]),
                writes=((b_smp[sp], c.psg) if h == 0 else (b_smp[sp],)))
            c.emit(lambda sp=sp: VE.reduce_max(sc1_sb.ap(),
                                               SMP[sp][0:1, 0:QSAMP],
                                               axis=AXX),
                   VE, "ve", "dv", reads=(b_smp[sp],), writes=(b_sc1,))
            if h == 0:
                c.emit(lambda: VE.tensor_copy(sc2_sb.ap(), sc1_sb.ap()),
                       VE, "ve", "dv", reads=(b_sc1,), writes=(b_sc2,))
            else:
                c.emit(lambda: VE.tensor_max(sc2_sb.ap(), sc2_sb.ap(),
                                             sc1_sb.ap()),
                       VE, "ve", "dv", reads=(b_sc1,), writes=(b_sc2,))
        c.psg = c.snap_guard()
    c.emit(lambda: VE.tensor_scalar(sc1_sb.ap(), sc2_sb.ap(), cfg.S_SCALE,
                                    C_MARGIN, ALU.mult, ALU.add),
           VE, "ve", "dv", reads=(b_sc2,), writes=(b_sc1,))
    c.emit(lambda: VE.tensor_scalar(sc1_sb.ap(), sc1_sb.ap(), -1.0, None,
                                    ALU.mult),
           VE, "ve", "dv", writes=(b_sc1,))
    with nc.psum_tensor("ncp", [128, 512], F32) as NCP:
        b_ncp = Buf()
        c.emit(lambda: TE.matmul(NCP[:, 0:1], ones_row_f.ap(), sc1_sb.ap(),
                                 start=True, stop=True),
               TE, "te", "pe", reads=(b_sc1, b_ones), writes=(b_ncp, c.psg))
        c.emit(lambda: VE.tensor_copy(negc_sb.ap(), NCP[:, 0:1]),
               VE, "ve", "dv", reads=(b_ncp,), writes=(b_negc,))
        c.psg = c.snap_guard()

    c.barrier_all(ENGS)
    es_ks.close()

    es_rmain.close()
    attn_sb = SB("attn_sb", [128, NH * TLOC], BF16, "right")
    Q_buf = [SB(f"Q_buf{i}", [128, TLOC], BF16, "right", es_qa)
             for i in range(2)]
    K_buf = [SB(f"K_buf{i}", [128, TLOC], BF16, "right", es_qa)
             for i in range(3)]
    V_buf = [SB(f"V_buf{i}", [128, NTC * 128], BF16, "right", es_qa)
             for i in range(3)]
    P_sb = [SB(f"P_sb{i}", [128, QHS], BF16, "right", es_qa)
            for i in range(3)]
    racc = SB("racc", [128, 2 * TLOC], F32, "right", es_qa)
    rb_sb = [SB(f"rb_sb{i}", [128, QHS], F32, "right", es_qa)
             for i in range(2)]
    recip_sb = [SB(f"recip_sb{i}", [1, QHS], F32, "right", es_qa)
                for i in range(2)]

    b_qbuf = [Buf(), Buf()]
    b_kbuf = [Buf(), Buf(), Buf()]
    b_vbuf = [Buf(), Buf(), Buf()]
    b_psb = [Buf(), Buf(), Buf()]
    b_racc = [Buf(), Buf(), Buf(), Buf()]
    b_rb = [Buf(), Buf()]
    b_recip = [Buf(), Buf()]
    b_attn = Buf()

    v_view = V_loc.rearrange("(r tc p) e -> r p tc e", tc=NTC, p=128)

    with nc.psum_tensor("S_ps0", [128, max(512, QHS)], F32) as Sp0, \
         nc.psum_tensor("S_ps1", [128, max(512, QHS)], F32) as Sp1, \
         nc.psum_tensor("O_ps0", [128, max(512, QHS)], F32) as Op0, \
         nc.psum_tensor("O_ps1", [128, max(512, QHS)], F32) as Op1:
        S_ps = [Sp0, Sp1]
        O_ps = [Op0, Op1]
        b_sps = [Buf(), Buf()]
        b_ops = [Buf(), Buf()]
        NBLK = NH * RG

        def emit_kv_load(blk):
            h, r = blk // RG, blk % RG
            pr = blk % 3
            c.emit_dma(lambda h=h, r=r, pr=pr: SY.dma_start(
                out=K_buf[pr].ap(),
                in_=K_loc[h * 128:(h + 1) * 128, r * TLOC:(r + 1) * TLOC]),
                SY, "sy", f"kb{pr}", reads=(b_kloc[0], b_kloc[1]),
                writes=(b_kbuf[pr],))
            c.emit_dma(lambda h=h, r=r, pr=pr: SY.dma_start(
                out=V_buf[pr].ap().rearrange("p (tc e) -> p tc e", e=128),
                in_=v_view[r, :, :, h * 128:(h + 1) * 128]),
                SY, "sy", f"vb{pr}", reads=(b_vloc[0], b_vloc[1]),
                writes=(b_vbuf[pr],))

        def emit_q_load(h):
            qp = h % 2
            c.emit_dma(lambda h=h, qp=qp: SY.dma_start(
                out=Q_buf[qp].ap(), in_=q_loc[h * 128:(h + 1) * 128, :]),
                SY, "sy", f"qb{qp}",
                reads=(b_qloc[0], b_qloc[1]), writes=(b_qbuf[qp],))

        emit_kv_load(0)
        emit_kv_load(1)
        emit_kv_load(2)
        emit_q_load(0)
        emit_q_load(1)

        pend = []

        def flush_one():
            if pend:
                pv_fn, racc_fn = pend.pop(0)
                pv_fn()
                racc_fn()

        def flush_pend():
            while pend:
                flush_one()

        u = 0
        for h in range(NH):
            for r in range(RG):
                blk = h * RG + r
                first_of_blk = True
                kb = blk % 3
                for qh in range(2):
                    for tc in range(NTC):
                        j = r * NTC + tc
                        up = u % 3
                        sp2 = u % 2

                        qp = h % 2

                        def mm_s(h=h, tc=tc, qh=qh, sp2=sp2, kb=kb, qp=qp):
                            last = None
                            for (q0, ql) in _sl(QHS):
                                last = TE.matmul(
                                    S_ps[sp2][:, q0:q0 + ql],
                                    K_buf[kb][:, tc * 128:(tc + 1) * 128],
                                    Q_buf[qp][:, qh * QHS + q0:
                                              qh * QHS + q0 + ql],
                                    start=True, stop=True)
                            return last
                        c.emit(mm_s, TE, "te", "pe",
                               reads=(b_kbuf[kb], b_qbuf[qp]),
                               writes=((b_sps[sp2], c.psg) if u == 0
                                       else (b_sps[sp2],)))
                        c.emit(lambda up=up, sp2=sp2: SC.activation(
                            P_sb[up].ap(), S_ps[sp2][:, 0:QHS], AF.Exp,
                            bias=negc_sb.ap(), scale=cfg.S_SCALE),
                            SC, "sc", "ac", reads=(b_sps[sp2], b_negc),
                            writes=(b_psb[up],))
                        if len(pend) >= 2:
                            flush_one()
                        if first_of_blk:
                            first_of_blk = False
                            if 3 <= blk + 2 < NBLK + 1 and blk + 2 < NBLK:
                                flush_pend()
                                emit_kv_load(blk + 2)

                        def pv_fn(j=j, tc=tc, qh=qh, up=up, kb=kb):
                            def mm_pv():
                                last = None
                                for (q0, ql) in _sl(QHS):
                                    last = TE.matmul(
                                        O_ps[qh][:, q0:q0 + ql],
                                        V_buf[kb][:, tc * 128:(tc + 1) * 128],
                                        P_sb[up][:, q0:q0 + ql],
                                        start=(j == 0),
                                        stop=(j == RG * NTC - 1))
                                return last
                            c.emit(mm_pv, TE, "te", "pe",
                                   reads=(b_vbuf[kb], b_psb[up]),
                                   writes=(b_ops[qh],))

                        if (u % 4) != 3:
                            eng, ekey, sem = VE, "ve", "dv"
                        else:
                            eng, ekey, sem = GP, "pl", "pl"

                        ai = qh * 2 + (j % 2)

                        def racc_fn(eng=eng, ekey=ekey, sem=sem, j=j, ai=ai,
                                    up=up):
                            sl_ = racc[:, ai * QHS:(ai + 1) * QHS]
                            if j < 2:
                                c.emit(lambda: eng.tensor_copy(
                                    sl_, P_sb[up].ap()),
                                    eng, ekey, sem, reads=(b_psb[up],),
                                    writes=(b_racc[ai],))
                            else:
                                c.emit(lambda: eng.tensor_add(
                                    sl_, sl_, P_sb[up].ap()),
                                    eng, ekey, sem, reads=(b_psb[up],),
                                    writes=(b_racc[ai],))

                        pend.append((pv_fn, racc_fn))
                        u += 1
            flush_pend()
            if h + 2 < NH:
                emit_q_load(h + 2)

            def st_rs(qh, h=h):
                sp = qh

                def mm_rs():
                    last = None
                    for (q0, ql) in _sl(QHS):
                        for pi in range(2):
                            ai = qh * 2 + pi
                            last = TE.matmul(S_ps[sp][0:1, q0:q0 + ql],
                                             ones_col_f.ap(),
                                             racc[:, ai * QHS + q0:
                                                  ai * QHS + q0 + ql],
                                             start=(pi == 0), stop=(pi == 1))
                    return last
                c.emit(mm_rs, TE, "te", "pe",
                       reads=(b_racc[qh * 2], b_racc[qh * 2 + 1], b_ones),
                       writes=(b_sps[sp],))

            def st_cb(qh, h=h):
                sp = qh
                c.emit(lambda: SC.activation(
                    recip_sb[qh].ap(), S_ps[sp][0:1, 0:QHS], AF.Ln,
                    bias=zer_sb.ap()),
                    SC, "sc", "ac", reads=(b_sps[sp], b_ones),
                    writes=(b_recip[qh],))
                c.emit(lambda: SC.activation(
                    recip_sb[qh].ap(), recip_sb[qh].ap(), AF.Exp,
                    scale=-1.0),
                    SC, "sc", "ac", reads=(b_recip[qh],),
                    writes=(b_recip[qh],))

                def mm_bc():
                    last = None
                    for (q0, ql) in _sl(QHS):
                        last = TE.matmul(S_ps[sp][:, q0:q0 + ql],
                                         ones_row_f.ap(),
                                         recip_sb[qh][0:1, q0:q0 + ql],
                                         start=True, stop=True)
                    return last
                c.emit(mm_bc, TE, "te", "pe", reads=(b_recip[qh], b_ones),
                       writes=(b_sps[sp],))

            def st_rm(qh, h=h):
                sp = qh
                c.emit(lambda: VE.tensor_copy(
                    rb_sb[qh].ap(), S_ps[sp][:, 0:QHS]),
                    VE, "ve", "dv", reads=(b_sps[sp],), writes=(b_rb[qh],))
                dst = attn_sb[:, h * TLOC + qh * QHS:
                              h * TLOC + (qh + 1) * QHS]
                c.emit(lambda dst=dst: VE.tensor_mul(
                    dst, O_ps[qh][:, 0:QHS], rb_sb[qh].ap()),
                    VE, "ve", "dv", reads=(b_ops[qh], b_rb[qh]),
                    writes=(b_attn,))

            st_rs(0)
            st_rs(1)
            st_cb(0)
            st_cb(1)
            st_rm(0)
            st_rm(1)
        c.psg = c.snap_guard()

    es_qa.close()
    c.barrier_all(ENGS)

    x2_sb2 = [SB(f"x2_sb{i}", [128, DC * TTS], F32, "left")
              for i in range(2)]
    x2q_sb = SB("x2q_sb", [128, DC * TTS], BF16, "left")
    acc_sb = SB("acc_sb", [128, TTS], F32, "left")
    acc2_sb = SB("acc2_sb", [128, TTS], F32, "left")
    sqt_sb = SB("sqt_sb", [128, TTS], F32, "left")
    qt1_sb = SB("qt1_sb", [128, TTS], BF16, "left")
    qt2_sb = SB("qt2_sb", [128, TTS], BF16, "left")
    xres = [SB(f"xres{i}", [128, TTS], F32, "left") for i in range(2)]
    ab_sb = SB("ab_sb", [128, TTS], F32, "left")
    bb_sb = SB("bb_sb", [128, TTS], F32, "left")
    m_sb = SB("m_sb", [1, TTS], F32, "left")
    v_sb = SB("v_sb", [1, TTS], F32, "left")
    r_sb = SB("r_sb", [1, TTS], F32, "left")
    bsm_sb = SB("bsm_sb", [1, TTS], F32, "left")

    b_x22 = [[Buf() for _ in range(DC)] for _ in range(2)]
    b_x2q = Buf()
    b_acc, b_acc2, b_sqt = Buf(), Buf(), Buf()
    b_qt1, b_qt2 = Buf(), Buf()
    b_xres = [Buf(), Buf()]
    b_ab, b_bb = Buf(), Buf()
    b_m, b_v, b_r, b_bsm = Buf(), Buf(), Buf(), Buf()
    b_outb = [Buf(), Buf()]

    with nc.psum_tensor("M_ps0", [128, max(512, TTS)], F32) as Mp0, \
         nc.psum_tensor("M_ps1", [128, max(512, TTS)], F32) as Mp1, \
         nc.psum_tensor("L_ps0", [128, max(512, TTS)], F32) as Lp0, \
         nc.psum_tensor("L_ps1", [128, max(512, TTS)], F32) as Lp1:
        M_ps = [Mp0, Mp1]
        L_ps = [Lp0, Lp1]
        b_mps = [Buf(), Buf()]
        b_lps = [Buf(), Buf()]

        def ln_finalize():
            def mm_sum():
                last = None
                for (t0, tl) in _sl(TTS):
                    last = TE.matmul(L_ps[0][0:1, t0:t0 + tl],
                                     ones_col_f.ap(), acc_sb[:, t0:t0 + tl],
                                     start=True, stop=True)
                return last

            def mm_sum2():
                last = None
                for (t0, tl) in _sl(TTS):
                    last = TE.matmul(L_ps[1][0:1, t0:t0 + tl],
                                     ones_col_f.ap(), acc2_sb[:, t0:t0 + tl],
                                     start=True, stop=True)
                return last
            c.emit(mm_sum, TE, "te", "pe", reads=(b_acc, b_ones),
                   writes=(b_lps[0],))
            c.emit(mm_sum2, TE, "te", "pe", reads=(b_acc2, b_ones),
                   writes=(b_lps[1],))
            c.emit(lambda: VE.tensor_scalar(m_sb.ap(), L_ps[0][0:1, 0:TTS],
                                            1.0 / D, None, ALU.mult),
                   VE, "ve", "dv", reads=(b_lps[0],), writes=(b_m,))
            c.emit(lambda: VE.tensor_scalar(v_sb.ap(), L_ps[1][0:1, 0:TTS],
                                            1.0 / D, None, ALU.mult),
                   VE, "ve", "dv", reads=(b_lps[1],), writes=(b_v,))
            c.emit(lambda: VE.tensor_mul(bsm_sb.ap(), m_sb.ap(), m_sb.ap()),
                   VE, "ve", "dv", reads=(b_m,), writes=(b_bsm,))
            c.emit(lambda: VE.tensor_sub(v_sb.ap(), v_sb.ap(), bsm_sb.ap()),
                   VE, "ve", "dv", reads=(b_bsm,), writes=(b_v,))
            c.emit(lambda: SC.activation(r_sb.ap(), v_sb.ap(), AF.Ln,
                                         bias=eps_sb.ap()),
                   SC, "sc", "ac", reads=(b_v, b_ones), writes=(b_r,))
            c.emit(lambda: SC.activation(r_sb.ap(), r_sb.ap(), AF.Exp,
                                         scale=-0.5),
                   SC, "sc", "ac", reads=(b_r,), writes=(b_r,))
            c.emit(lambda: VE.scalar_tensor_tensor(
                bsm_sb.ap(), m_sb.ap(), -1.0, r_sb.ap(), ALU.mult, ALU.mult),
                VE, "ve", "dv", reads=(b_m, b_r), writes=(b_bsm,))

            def mm_ba():
                last = None
                for (t0, tl) in _sl(TTS):
                    last = TE.matmul(L_ps[0][:, t0:t0 + tl], ones_row_f.ap(),
                                     r_sb[0:1, t0:t0 + tl],
                                     start=True, stop=True)
                return last

            def mm_bb():
                last = None
                for (t0, tl) in _sl(TTS):
                    last = TE.matmul(L_ps[1][:, t0:t0 + tl], ones_row_f.ap(),
                                     bsm_sb[0:1, t0:t0 + tl],
                                     start=True, stop=True)
                return last
            c.emit(mm_ba, TE, "te", "pe", reads=(b_r, b_ones),
                   writes=(b_lps[0],))
            c.emit(mm_bb, TE, "te", "pe", reads=(b_bsm, b_ones),
                   writes=(b_lps[1],))
            c.emit(lambda: SC.activation(ab_sb.ap(), L_ps[0][:, 0:TTS],
                                         AF.Copy),
                   SC, "sc", "ac", reads=(b_lps[0],), writes=(b_ab,))
            c.emit(lambda: SC.activation(bb_sb.ap(), L_ps[1][:, 0:TTS],
                                         AF.Copy),
                   SC, "sc", "ac", reads=(b_lps[1],), writes=(b_bb,))

        for tt in range(NTT):
            x2_sb = x2_sb2[tt % 2]
            b_x2 = b_x22[tt % 2]
            for ec in range(DC):
                pr = ec % 2
                ps3 = ec % 3
                c.emit_dma(lambda ec=ec, ps3=ps3: SY.dma_start(
                    out=wstage[ps3].ap(), in_=wo[:, ec * D:(ec + 1) * D]),
                    SY, "sy", f"wst{ps3}", writes=(b_wstage[ps3],))
                c.emit(lambda pr=pr, ps3=ps3: SC.activation(
                    wbf[pr].ap(), wstage[ps3].ap(), AF.Copy),
                    SC, "sc", "ac", reads=(b_wstage[ps3],),
                    writes=(b_wbf[pr],))
                c.emit_dma(lambda ec=ec, tt=tt, pr=pr: SC.dma_start(
                    out=xres[pr].ap(),
                    in_=xT[ec * 128:(ec + 1) * 128,
                           tt * TTS:(tt + 1) * TTS]),
                    SC, "sc", f"xrs{pr}", writes=(b_xres[pr],))

                def mm(ec=ec, tt=tt, pr=pr):
                    last = None
                    for dc in range(DC):
                        lhsT = wbf[pr][:, dc * 128:(dc + 1) * 128]
                        for (t0, tl) in _sl(TTS):
                            last = TE.matmul(
                                M_ps[pr][:, t0:t0 + tl], lhsT,
                                attn_sb[:, dc * TLOC + tt * TTS + t0:
                                        dc * TLOC + tt * TTS + t0 + tl],
                                start=(dc == 0), stop=(dc == DC - 1))
                    return last
                c.emit(mm, TE, "te", "pe", reads=(b_wbf[pr], b_attn),
                       writes=((b_mps[pr], c.psg) if tt == 0 and ec == 0
                               else (b_mps[pr],)))
                dst = x2_sb[:, ec * TTS:(ec + 1) * TTS]
                c.emit(lambda pr=pr, dst=dst: VE.tensor_add(
                    dst, M_ps[pr][:, 0:TTS], xres[pr].ap()),
                    VE, "ve", "dv", reads=(b_mps[pr], b_xres[pr]),
                    writes=(b_x2[ec],))
                if ec == 0:
                    c.emit(lambda dst=dst: VE.tensor_copy(acc_sb.ap(), dst),
                           VE, "ve", "dv", reads=(b_x2[ec],), writes=(b_acc,))
                else:
                    c.emit(lambda dst=dst: VE.tensor_add(acc_sb.ap(),
                                                         acc_sb.ap(), dst),
                           VE, "ve", "dv", reads=(b_x2[ec],), writes=(b_acc,))
                c.emit(lambda dst=dst: VE.tensor_mul(sqt_sb.ap(), dst, dst),
                       VE, "ve", "dv", reads=(b_x2[ec],), writes=(b_sqt,))
                if ec == 0:
                    c.emit(lambda: VE.tensor_copy(acc2_sb.ap(), sqt_sb.ap()),
                           VE, "ve", "dv", reads=(b_sqt,), writes=(b_acc2,))
                else:
                    c.emit(lambda: VE.tensor_add(acc2_sb.ap(), acc2_sb.ap(),
                                                 sqt_sb.ap()),
                           VE, "ve", "dv", reads=(b_sqt,), writes=(b_acc2,))
            ln_finalize()
            for ec in range(DC):
                dst = x2_sb[:, ec * TTS:(ec + 1) * TTS]
                c.emit(lambda dst=dst: VE.tensor_mul(dst, dst, ab_sb.ap()),
                       VE, "ve", "dv", reads=(b_ab,), writes=(b_x2[ec],))
                c.emit(lambda dst=dst: VE.tensor_add(dst, dst, bb_sb.ap()),
                       VE, "ve", "dv", reads=(b_bb,), writes=(b_x2[ec],))
                c.emit(lambda dst=dst: SC.activation(
                    qt1_sb.ap(), dst, AF.Sign, bias=xthn_sb.ap()),
                    SC, "sc", "ac", reads=(b_x2[ec], b_ones), writes=(b_qt1,))
                c.emit(lambda dst=dst: SC.activation(
                    qt2_sb.ap(), dst, AF.Sign, bias=xthp_sb.ap()),
                    SC, "sc", "ac", reads=(b_x2[ec], b_ones), writes=(b_qt2,))
                c.emit(lambda: VE.tensor_add(qt1_sb.ap(), qt1_sb.ap(),
                                             qt2_sb.ap()),
                       VE, "ve", "dv", reads=(b_qt2,), writes=(b_qt1,))
                qdst = x2q_sb[:, ec * TTS:(ec + 1) * TTS]
                c.emit(lambda qdst=qdst: SC.activation(
                    qdst, qt1_sb.ap(), AF.Copy, scale=SC_X / 2.0),
                    SC, "sc", "ac", reads=(b_qt1,), writes=(b_x2q,))

            for oc in range(DC):
                pr = oc % 2
                c.emit_dma(lambda oc=oc, pr=pr: SY.dma_start(
                    out=wbf[pr].ap(), in_=wr_loc[:, oc * D:(oc + 1) * D]),
                    SY, "sy", f"wrb{pr}", reads=(b_wrloc,),
                    writes=(b_wbf[pr],))

                def mm(oc=oc, pr=pr):
                    last = None
                    for dc in range(DC):
                        lhsT = wbf[pr][:, dc * 128:(dc + 1) * 128]
                        for (t0, tl) in _sl(TTS):
                            last = TE.matmul(
                                M_ps[pr][:, t0:t0 + tl], lhsT,
                                x2q_sb[:, dc * TTS + t0: dc * TTS + t0 + tl],
                                start=(dc == 0), stop=(dc == DC - 1))
                    return last
                c.emit(mm, TE, "te", "pe", reads=(b_wbf[pr], b_x2q),
                       writes=(b_mps[pr],))
                dst = x2_sb[:, oc * TTS:(oc + 1) * TTS]
                c.emit(lambda pr=pr, dst=dst: VE.tensor_add(
                    dst, M_ps[pr][:, 0:TTS], dst),
                    VE, "ve", "dv", reads=(b_mps[pr],), writes=(b_x2[oc],))
                if oc == 0:
                    c.emit(lambda dst=dst: VE.tensor_copy(acc_sb.ap(), dst),
                           VE, "ve", "dv", reads=(b_x2[oc],), writes=(b_acc,))
                else:
                    c.emit(lambda dst=dst: VE.tensor_add(acc_sb.ap(),
                                                         acc_sb.ap(), dst),
                           VE, "ve", "dv", reads=(b_x2[oc],), writes=(b_acc,))
                c.emit(lambda dst=dst: VE.tensor_mul(sqt_sb.ap(), dst, dst),
                       VE, "ve", "dv", reads=(b_x2[oc],), writes=(b_sqt,))
                if oc == 0:
                    c.emit(lambda: VE.tensor_copy(acc2_sb.ap(), sqt_sb.ap()),
                           VE, "ve", "dv", reads=(b_sqt,), writes=(b_acc2,))
                else:
                    c.emit(lambda: VE.tensor_add(acc2_sb.ap(), acc2_sb.ap(),
                                                 sqt_sb.ap()),
                           VE, "ve", "dv", reads=(b_sqt,), writes=(b_acc2,))
            ln_finalize()
            for oc in range(DC):
                dst = x2_sb[:, oc * TTS:(oc + 1) * TTS]
                c.emit(lambda dst=dst: VE.tensor_mul(dst, dst, ab_sb.ap()),
                       VE, "ve", "dv", reads=(b_ab,), writes=(b_x2[oc],))
                c.emit(lambda dst=dst: VE.tensor_add(dst, dst, bb_sb.ap()),
                       VE, "ve", "dv", reads=(b_bb,), writes=(b_x2[oc],))
                c.emit_dma(lambda oc=oc, tt=tt, dst=dst: GP.dma_start(
                    out=outT[oc * 128:(oc + 1) * 128,
                             tt * TTS:(tt + 1) * TTS],
                    in_=dst),
                    GP, "gq", f"out{oc % 2}", reads=(b_x2[oc],),
                    writes=(b_outb[oc % 2],))

    c.barrier_all(ENGS)
    es.close()
    return nc


def _reorder_hblock(WT, DC):
    d, e = WT.shape
    nh = e // 128
    a = WT.reshape(DC, 128, nh, 128).transpose(1, 2, 0, 3)
    return np.ascontiguousarray(a.reshape(128, nh * DC * 128))


def _reorder_plain(WT, DC):
    d, e = WT.shape
    a = WT.reshape(DC, 128, e).transpose(1, 0, 2)
    return np.ascontiguousarray(a.reshape(128, DC * e))


def kernel(L0_attn_x, L0_attn_Wq, L0_attn_Wk, L0_attn_Wv, L0_attn_Wo,
           L0_ffn_weight, L0_ffn_rram_variation):
    cfg = Cfg(NH=16, TLOC=2048, RG=4)
    x = np.asarray(L0_attn_x, dtype=np.float32)
    B, T, Dm = x.shape
    f32 = lambda a: np.asarray(a, dtype=np.float32)
    ins = {
        "wq": _reorder_hblock(f32(L0_attn_Wq).T, cfg.DC),
        "wk": _reorder_hblock(f32(L0_attn_Wk).T, cfg.DC),
        "wv": _reorder_plain(f32(L0_attn_Wv).T, cfg.DC),
        "wo": _reorder_hblock(f32(L0_attn_Wo).T, cfg.DC),
        "wf": _reorder_hblock(f32(L0_ffn_weight).T, cfg.DC),
        "wr": _reorder_hblock(f32(L0_ffn_rram_variation).T, cfg.DC),
    }
    nc = build(cfg)
    xkTs = [np.ascontiguousarray(x[b].T) for b in range(B)]
    in_maps = []
    ncores = 2 * cfg.RG
    for core in range(ncores):
        b, sl = core // cfg.RG, core % cfg.RG
        m = dict(ins)
        m["xT"] = np.ascontiguousarray(
            x[b, sl * cfg.TLOC:(sl + 1) * cfg.TLOC, :].T)
        m["xkT"] = xkTs[b]
        in_maps.append(m)
    res = run_bass_kernel_spmd(nc, in_maps, core_ids=list(range(ncores)))
    out = np.empty((B, T, Dm), dtype=np.float32)
    for core in range(ncores):
        b, sl = core // cfg.RG, core % cfg.RG
        out[b, sl * cfg.TLOC:(sl + 1) * cfg.TLOC, :] = \
            res.results[core]["outT"].T
    return out
